# version 45
# baseline (speedup 1.0000x reference)
import math
import contextlib
import numpy as np

import concourse.bass as bass
import concourse.mybir as mybir
from concourse.bass_utils import run_bass_kernel_spmd

F32 = mybir.dt.float32
BF16 = mybir.dt.bfloat16
I32 = mybir.dt.int32
AF = mybir.ActivationFunctionType
ALU = mybir.AluOpType
AX = mybir.AxisListType

D = 1024
H = 16
DK = 32
DV = 64
KD = 512
LOW = 16
EPS = 1e-5
LN_S = math.log(DK ** -0.5)


class Sem:
    def __init__(self, h):
        self.h = h
        self.n = 0

    def bump(self, k=1):
        self.n += k
        return self.n


class Step:
    __slots__ = ("eng", "emit", "incs", "needs_inc", "token", "drain",
                 "waits_sym", "waits_extra", "ord", "block", "compl_dma")

    def __init__(self, eng, emit, incs, ordinal, block):
        self.eng = eng
        self.emit = emit
        self.incs = incs
        self.needs_inc = False
        self.token = None
        self.drain = False
        self.waits_sym = []
        self.waits_extra = []
        self.ord = ordinal
        self.block = block
        self.compl_dma = None


class Builder2:

    ENGINES = ("tensor", "vector", "scalar", "gpsimd", "sync")
    ENG_SEM = {"tensor": "sT", "vector": "sV", "scalar": "sS",
               "gpsimd": "sG", "sync": "sY"}

    def __init__(self, nc, sems):
        self.nc = nc
        self.sems = sems
        self.key = {}
        self.ordn = 0
        self.block_idx = 0
        self.streams = {e: [] for e in self.ENGINES}
        self.eng_drained_upto = {e: -1 for e in self.ENGINES}

    def step(self, eng, emit, reads=(), writes=(), incs=(), waits=(),
             dma_sem=None):
        il = []
        for sname, k in incs:
            s = self.sems[sname]
            s.bump(k)
            il.append((s.h, k, s.n))
        st = Step(eng, emit, il, self.ordn, self.block_idx)
        self.ordn += 1
        st.waits_extra = list(waits)

        deps = []
        need_drain = False
        for k in reads:
            ks = self.key.get(k)
            if ks is not None and ks["w"] is not None:
                deps.append(ks["w"])
        for k in writes:
            ks = self.key.get(k)
            if ks is not None:
                if ks["r"]:
                    deps.extend(ks["r"])
                elif ks["w"] is not None:
                    deps.append(ks["w"])
        for d in deps:
            if d.compl_dma is not None:
                sn, val = d.compl_dma
                st.waits_extra.append((sn, max(val, self.sems[sn].n)))
            elif d.eng == eng:
                if eng != "tensor" and d.ord > self.eng_drained_upto[eng]:
                    need_drain = True
            else:
                st.waits_sym.append(d)
        if need_drain:
            st.drain = True
            self.eng_drained_upto[eng] = st.ord - 1
        if dma_sem is not None:
            sm = self.sems[dma_sem]
            sm.bump(16)
            st.compl_dma = (dma_sem, sm.n)
        for k in reads:
            ks = self.key.setdefault(k, {"w": None, "r": []})
            ks["r"].append(st)
        for k in writes:
            self.key[k] = {"w": st, "r": []}
        self.streams[eng].append(st)
        return st

    def flush(self, name=None):
        nc = self.nc
        for eng in self.ENGINES:
            for st in self.streams[eng]:
                for d in st.waits_sym:
                    if d.compl_dma is None and not d.needs_inc:
                        if d.block == self.block_idx:
                            d.needs_inc = True
        for eng in self.ENGINES:
            s = self.sems[self.ENG_SEM[eng]]
            for st in self.streams[eng]:
                if st.needs_inc:
                    s.bump(1)
                    st.token = s.n
        with nc.Block(name) as block:
            for eng in self.ENGINES:
                ops = self.streams[eng]
                if not ops:
                    continue

                def body(e, ops=ops, eng=eng):
                    hw = {}
                    for st in ops:
                        if st.drain:
                            e.drain()
                        wl = {}
                        for d in st.waits_sym:
                            if d.block < st.block:
                                continue
                            sn, val = self.ENG_SEM[d.eng], d.token
                            assert val is not None
                            if wl.get(sn, 0) < val:
                                wl[sn] = val
                        for sn, val in st.waits_extra:
                            if wl.get(sn, 0) < val:
                                wl[sn] = val
                        for sn, val in wl.items():
                            if hw.get(sn, 0) < val:
                                hw[sn] = val
                                e.wait_ge(self.sems[sn].h, val)
                        instr = st.emit(e)
                        if st.compl_dma is not None:
                            assert instr is not None
                            instr = instr.then_inc(
                                self.sems[st.compl_dma[0]].h, 16)
                        if st.needs_inc:
                            assert instr is not None
                            instr = instr.then_inc(
                                self.sems[self.ENG_SEM[eng]].h, 1)
                        for h, k, _ in st.incs:
                            assert instr is not None
                            instr = instr.then_inc(h, k)

                getattr(block, eng)(body)
        self.streams = {e: [] for e in self.ENGINES}
        self.block_idx += 1
        for e in self.ENGINES:
            self.eng_drained_upto[e] = self.ordn - 1


def build_program(ntm=16, ntp=6, rt=2):
    assert ntm % rt == 0 and ntp % rt == 0
    R = rt * 128
    nc = bass.Bass("TRN2", target_bir_lowering=False)

    rows_in = (ntm + ntp) * 128
    x_d = nc.dram_tensor("x", [rows_in, D], F32, kind="ExternalInput")
    y_d = nc.dram_tensor("y", [ntm * 128, D], F32, kind="ExternalOutput")
    wq_d = nc.dram_tensor("wq", [D, KD], F32, kind="ExternalInput")
    wk_d = nc.dram_tensor("wk", [D, KD], F32, kind="ExternalInput")
    wv_d = nc.dram_tensor("wv", [D, D], F32, kind="ExternalInput")
    wgk1_d = nc.dram_tensor("wgk1", [D, LOW], F32, kind="ExternalInput")
    wgk2_d = nc.dram_tensor("wgk2", [LOW, KD], F32, kind="ExternalInput")
    bgk2_d = nc.dram_tensor("bgk2", [KD], F32, kind="ExternalInput")
    wg_d = nc.dram_tensor("wg", [D, D], F32, kind="ExternalInput")
    gnw_d = nc.dram_tensor("gnw", [DV], F32, kind="ExternalInput")
    wo_d = nc.dram_tensor("wo", [D, D], F32, kind="ExternalInput")
    wf_d = nc.dram_tensor("wf", [D, D], F32, kind="ExternalInput")
    wsp = {}
    for nm, ncol in (("wq", KD), ("wk", KD), ("wv", D), ("wg", D), ("wo", D)):
        wsp[nm] = (nc.dram_tensor(nm + "h", [D, ncol], BF16),
                   nc.dram_tensor(nm + "l", [D, ncol], BF16))

    ctx = contextlib.ExitStack()
    sb = lambda name, shape, dt: ctx.enter_context(nc.sbuf_tensor(name, shape, dt))
    ps = lambda name, shape, dt: ctx.enter_context(nc.psum_tensor(name, shape, dt))
    sems = {}
    for sname in ("sT", "sV", "sS", "sG", "sY", "dW", "dW2", "dW3", "dX", "dX2",
                  "dXq", "dO", "dM", "dS", "dS2"):
        sems[sname] = Sem(ctx.enter_context(nc.semaphore(sname)))

    XR = sb("XR", [128, ntm * D], F32)
    WH = sb("WH", [128, 8 * D], BF16)
    WL = sb("WL", [128, 8 * D], BF16)

    QFFN = sb("QFFN", [128, 8 * D], BF16)
    WGK1H = sb("WGK1H", [128, 8 * LOW], BF16)
    WGK1L = sb("WGK1L", [128, 8 * LOW], BF16)
    WGK2 = sb("WGK2", [LOW + 1, KD], F32)
    NEGB = sb("NEGB", [128, 4], F32)
    GNP = sb("GNP", [128, 1], F32)
    IDF = sb("IDF", [128, 128], F32)
    MASK4 = sb("MASK4", [128, 512], BF16)
    ABC = sb("ABC", [128, D], F32)
    HS = sb("HS", [128, 4 * DV], F32)
    U2TH = sb("U2TH", [128, 8 * R], BF16)
    U2TL = sb("U2TL", [128, 8 * R], BF16)
    QT = sb("QT", [128, 4 * R], F32)
    KT = sb("KT", [128, 4 * R], F32)
    GKT = sb("GKT", [128, 4 * R], F32)
    BT = GKT
    Z1T = sb("Z1T", [LOW + 1, R], F32)
    VN = sb("VN", [128, rt * D], F32)
    GN = sb("GN", [128, rt * D], F32)
    OF2 = VN
    YSC = sb("YSC", [128, D], F32)
    DSC = sb("DSC", [128, D], F32)
    ISC = sb("ISC", [128, D], mybir.dt.int16)
    C2SC = sb("C2SC", [128, D], BF16)
    T1SC = sb("T1SC", [128, D], BF16)
    T1B2 = sb("T1B2", [128, 2 * D], BF16)
    IDFB = sb("IDFB", [128, 128], BF16)
    UQT = sb("UQT", [128, 8 * 128], BF16)
    SCD = sb("SCD", [128, D], F32)
    OSB = YSC
    OSQ = DSC
    OFTH = sb("OFTH", [128, 8 * 128], BF16)
    OFTL = sb("OFTL", [128, 8 * 128], BF16)
    EXX = [sb("EXX0", [128, 128], F32), sb("EXX1", [128, 128], F32)]
    QTE = [sb("QTE0", [128, 128], F32), sb("QTE1", [128, 128], F32)]
    KHT = [sb("KHT0", [128, 128], F32), sb("KHT1", [128, 128], F32)]
    KHN = [sb("KHN0", [128, 128], F32), sb("KHN1", [128, 128], F32)]
    QX = [sb("QX0", [128, 512], F32), sb("QX1", [128, 512], F32)]
    HSX = [sb("HSX0", [128, 256], F32), sb("HSX1", [128, 256], F32)]
    AM = [sb("AM0", [128, 512], F32), sb("AM1", [128, 512], F32)]
    QMASK4 = sb("QMASK4", [128, 512], BF16)
    HMASK = sb("HMASK", [128, 256], BF16)
    ST = sb("ST", [128, 48], F32)
    BN = sb("BN", [128, 16], F32)
    MS16 = sb("MS16", [128, 16], F32)
    SC16 = sb("SC16", [128, 16], F32)
    SCT = sb("SCT", [LOW, 128], F32)
    CLNS = sb("CLNS", [128, 1], F32)
    setup_ctx = contextlib.ExitStack()
    ssb = lambda name, shape, dt: setup_ctx.enter_context(
        nc.sbuf_tensor(name, shape, dt))
    PIDXI = ssb("PIDXI", [128, 1], I32)
    PIDX = ssb("PIDX", [128, 1], F32)
    COLI = ssb("COLI", [128, 512], I32)
    AMX8 = ssb("AMX8", [128, 8], F32)
    RS8 = ssb("RS8", [128, 8], F32)
    CB32 = ssb("CB32", [128, 512], I32)
    CB32H = ssb("CB32H", [128, 256], I32)


    PT = ps("PT", [128, 512], F32)
    PA = ps("PA", [128, 512], F32)
    PO = ps("PO", [128, 1024], F32)
    PS_ = ps("PS_", [128, 256], F32)
    PP = ps("PP", [128, 512], F32)
    PF = ps("PF", [128, 1024], F32)

    STG = VN[:, 0:D]
    RSCB = GN[:, 0:D]
    WGK1 = GN[:, D:D + 8 * LOW]
    b = Builder2(nc, sems)
    C_MEAN, C_ISTD, C_AMX, C_SL, C_AA, C_BB, C_T0, C_T1, C_BH, C_DC = range(10)
    SB_ = [0]

    def stat(c):
        return ST[:, SB_[0] + c:SB_[0] + c + 1]

    def K_STf(i):
        return f"st{SB_[0] + i}"

    class _KST:
        def __getitem__(self, i):
            return K_STf(i)

    K_ST = _KST()

    def rsqrt_stat(src_c, dst_c, kin, kout):
        b.step("scalar", lambda e: e.activation(stat(C_T1), stat(src_c), AF.Ln),
               reads=[kin], writes=[K_ST[C_T1]])
        b.step("scalar", lambda e: e.activation(
            stat(dst_c), stat(C_T1), AF.Exp, scale=-0.5),
            reads=[K_ST[C_T1]], writes=[kout])

    def ln_stats(src_tile, src_keys, need_amax=True, scr=None):
        sc_t, sc_k = (DSC, "DSC") if scr is None else scr
        b.step("vector", lambda e: e.bn_stats(BN[:, 0:6], src_tile[:, 0:512]),
               reads=src_keys, writes=["bn.a"])
        b.step("vector", lambda e: e.bn_stats(BN[:, 6:12], src_tile[:, 512:1024]),
               reads=src_keys, writes=["bn.b"])
        b.step("vector", lambda e: e.bn_aggr(BN[:, 12:14], BN[:, 0:12]),
               reads=["bn.a", "bn.b"], writes=["bn.m"])
        b.step("vector", lambda e: e.tensor_copy(stat(C_MEAN), BN[:, 12:13]),
               reads=["bn.m"], writes=[K_ST[C_MEAN]])
        if need_amax:
            b.step("vector", lambda e: e.tensor_scalar(
                sc_t[:, :], src_tile, stat(C_MEAN), None, ALU.subtract),
                reads=src_keys + [K_ST[C_MEAN]], writes=[sc_k])
            b.step("vector", lambda e: e.tensor_reduce(
                stat(C_AMX), sc_t[:, :], AX.X, ALU.max,
                apply_absolute_value=True),
                reads=[sc_k], writes=[K_ST[C_AMX]])
        b.step("vector", lambda e: e.tensor_scalar(
            stat(C_T0), BN[:, 13:14], EPS, None, ALU.add),
            reads=["bn.m"], writes=[K_ST[C_T0]])
        rsqrt_stat(C_T0, C_ISTD, K_ST[C_T0], K_ST[C_ISTD])

    FQA = (YSC, "YSC", ISC, "ISC", C2SC, "C2SC", T1SC, 0, "T1SC")

    def FQB(il):
        return (SCD, "SCD", ISC, "ISC", C2SC, "C2SC",
                T1B2, il * D, f"T1B{il}")

    def emit_ffn_quant(res_tile, res_key, S=None, slc=None):
        Y, YK, I_, IK, C2, C2K, T1, T1o, T1K = S if S is not None else FQA
        slcol, slkey = (SB_[0] + C_SL, K_ST[C_SL]) if slc is None else slc
        ln_stats(res_tile, [res_key], scr=(Y, YK))
        b.step("vector", lambda e: e.tensor_tensor(
            stat(C_T0), stat(C_AMX), stat(C_ISTD), ALU.mult),
            reads=[K_ST[C_AMX], K_ST[C_ISTD]], writes=[K_ST[C_T0]])
        b.step("vector", lambda e, slcol=slcol: e.tensor_scalar(
            ST[:, slcol:slcol + 1], stat(C_T0), 1e-12, 1.0 / 127.0, ALU.max,
            op1=ALU.mult),
            reads=[K_ST[C_T0]], writes=[slkey])
        b.step("vector", lambda e, slcol=slcol: e.reciprocal(
            stat(C_T0), ST[:, slcol:slcol + 1]),
            reads=[slkey], writes=[K_ST[C_T0]])
        b.step("vector", lambda e: e.tensor_tensor(
            stat(C_AA), stat(C_ISTD), stat(C_T0), ALU.mult),
            reads=[K_ST[C_ISTD], K_ST[C_T0]], writes=[K_ST[C_AA]])
        b.step("vector", lambda e: e.tensor_tensor(
            stat(C_T1), stat(C_MEAN), stat(C_AA), ALU.mult),
            reads=[K_ST[C_MEAN], K_ST[C_AA]], writes=[K_ST[C_T1]])
        b.step("vector", lambda e: e.tensor_scalar(
            stat(C_BB), stat(C_T1), -1.0, None, ALU.mult),
            reads=[K_ST[C_T1]], writes=[K_ST[C_BB]])
        b.step("scalar", lambda e, Y=Y: e.activation(
            Y[:, :], res_tile, AF.Identity, bias=stat(C_BB), scale=stat(C_AA)),
            reads=[res_key, K_ST[C_BB], K_ST[C_AA]], writes=[YK])
        b.step("vector", lambda e, Y=Y, I_=I_: e.tensor_copy(I_[:, :], Y[:, :]),
               reads=[YK], writes=[IK])
        b.step("vector", lambda e, Y=Y, I_=I_, C2=C2: e.scalar_tensor_tensor(
            C2[:, :], Y[:, :], 0.5, I_[:, :], ALU.add, ALU.is_lt),
            reads=[YK, IK], writes=[C2K])
        b.step("vector", lambda e, Y=Y, I_=I_, T1=T1, T1o=T1o:
               e.scalar_tensor_tensor(
            T1[:, T1o:T1o + D], Y[:, :], -0.5, I_[:, :], ALU.add, ALU.is_gt),
            reads=[YK, IK], writes=[T1K])
        b.step("vector", lambda e, I_=I_, T1=T1, T1o=T1o: e.tensor_tensor(
            T1[:, T1o:T1o + D], I_[:, :], T1[:, T1o:T1o + D], ALU.add),
            reads=[IK, T1K], writes=[T1K])
        b.step("vector", lambda e, T1=T1, C2=C2, T1o=T1o: e.tensor_tensor(
            T1[:, T1o:T1o + D], T1[:, T1o:T1o + D], C2[:, :], ALU.subtract),
            reads=[T1K, C2K], writes=[T1K])

    def emit_ffn_apply(res_tile, res_key, S=None, slc=None, altp=False):
        Y, YK, I_, IK, C2, C2K, T1, T1o, T1K = S if S is not None else FQA
        slcol, slkey = (SB_[0] + C_SL, K_ST[C_SL]) if slc is None else slc
        tpb, tpk = (PA, "PA") if altp else (PT, "PT")
        mmb, mmk = (PO, ["PO0", "PO1"]) if altp else (PF, ["PF0", "PF1"])
        uq = OFTH if altp else UQT
        uqk = ["OFT0", "OFT1"] if altp else ["UQT0", "UQT1"]
        for half in range(2):
            for j in range(4):
                dt = half * 4 + j
                b.step("tensor", lambda e, dt=dt, j=j, T1=T1, T1o=T1o:
                       e.matmul(
                    tpb[:, j * 128:(j + 1) * 128],
                    T1[:, T1o + dt * 128:T1o + (dt + 1) * 128],
                    IDFB[:, :], start=True, stop=True),
                    reads=[T1K, "IDFB"], writes=[tpk])
            b.step("scalar", lambda e, half=half, uq=uq: e.copy(
                uq[:, half * 512:(half + 1) * 512], tpb[:, 0:512]),
                reads=[tpk], writes=[uqk[half]])
        for nb in range(2):
            for dt in range(8):
                b.step("tensor", lambda e, nb=nb, dt=dt, uq=uq: e.matmul(
                    mmb[:, nb * 512:(nb + 1) * 512],
                    uq[:, dt * 128:(dt + 1) * 128],
                    QFFN[:, dt * D + nb * 512: dt * D + (nb + 1) * 512],
                    start=(dt == 0), stop=(dt == 7)),
                    reads=[uqk[dt // 4], "QFFN"], writes=[mmk[nb]])
        b.step("vector", lambda e: e.tensor_tensor(
            DSC[:, :], mmb[:, 0:1024], ABC[:, :], ALU.mult),
            reads=mmk + ["ABC"], writes=["DSC"])
        b.step("vector", lambda e, slcol=slcol: e.scalar_tensor_tensor(
            res_tile, DSC[:, :], ST[:, slcol:slcol + 1], res_tile,
            ALU.mult, ALU.add),
            reads=["DSC", slkey, res_key], writes=[res_key])

    def emit_ffn(res_tile, res_key, S=None):
        emit_ffn_quant(res_tile, res_key, S)
        emit_ffn_apply(res_tile, res_key, S)

    def emit_ln2_u2t(res_tile, res_key, iloc):
        ln_stats(res_tile, [res_key], need_amax=False)
        b.step("vector", lambda e: e.tensor_tensor(
            stat(C_T0), stat(C_MEAN), stat(C_ISTD), ALU.mult),
            reads=[K_ST[C_MEAN], K_ST[C_ISTD]], writes=[K_ST[C_T0]])
        b.step("vector", lambda e: e.tensor_scalar(
            stat(C_BB), stat(C_T0), -1.0, None, ALU.mult),
            reads=[K_ST[C_T0]], writes=[K_ST[C_BB]])
        b.step("scalar", lambda e: e.activation(
            YSC[:, :], res_tile, AF.Identity, bias=stat(C_BB), scale=stat(C_ISTD)),
            reads=[res_key, K_ST[C_BB], K_ST[C_ISTD]], writes=["YSC"])
        for half in range(2):
            for j in range(4):
                dt = half * 4 + j
                b.step("tensor", lambda e, dt=dt, j=j: e.transpose(
                    PT[:, j * 128:(j + 1) * 128], YSC[:, dt * 128:(dt + 1) * 128],
                    IDF[:, :]), reads=["YSC", "IDF"], writes=["PT"])
            for j in range(4):
                dt = half * 4 + j
                sl = slice(dt * R + iloc * 128, dt * R + (iloc + 1) * 128)
                b.step("scalar", lambda e, j=j, sl=sl: e.copy(
                    U2TH[:, sl], PT[:, j * 128:(j + 1) * 128]),
                    reads=["PT"], writes=["U2TH"])
                b.step("vector", lambda e, j=j, sl=sl: e.tensor_tensor(
                    U2TL[:, sl], PT[:, j * 128:(j + 1) * 128], U2TH[:, sl],
                    ALU.subtract),
                    reads=["PT", "U2TH"], writes=["U2TL"])

    WQMAP = {"wq": (("sync", "dW"), ("gpsimd", "dW2")),
             "wk": (("gpsimd", "dW2"), ("sync", "dW")),
             "wv": (("sync", "dW"), ("gpsimd", "dW2")),
             "wg": (("sync", "dW"), ("gpsimd", "dW2")),
             "wo": (("sync", "dW"), ("gpsimd", "dW2"))}

    def load_weight(nm, nw, pro=False, half=0):
        (hiq, hisem), (loq, losem) = (
            (("sync", "dW"), ("sync", "dW")) if pro else WQMAP[nm])
        hi_d, lo_d = wsp[nm]
        g = nw // 512
        cb = half * 8 * 512
        gb = half * 8
        for dt in range(8):
            b.step(hiq, lambda e, dt=dt, nw=nw, hi_d=hi_d, cb=cb: e.dma_start(
                WH[:, cb + dt * nw:cb + (dt + 1) * nw],
                hi_d[dt * 128:(dt + 1) * 128, :]),
                reads=[f"{nm}h{dt}"],
                writes=[f"WBH{gb + dt * g + i}" for i in range(g)],
                dma_sem=hisem)
            b.step(loq, lambda e, dt=dt, nw=nw, lo_d=lo_d, cb=cb: e.dma_start(
                WL[:, cb + dt * nw:cb + (dt + 1) * nw],
                lo_d[dt * 128:(dt + 1) * 128, :]),
                reads=[f"{nm}l{dt}"],
                writes=[f"WBL{gb + dt * g + i}" for i in range(g)],
                dma_sem=losem)

    def proj_transposed(dst, dst_key, nw, half=0):
        combos = [(WH, U2TH, "H"), (WH, U2TL, "H"), (WL, U2TH, "L")]
        cb = half * 8 * 512
        gb = half * 8
        for mc in range(4):
            pb = PP if mc % 2 == 0 else PA
            ppk = "PP" if mc % 2 == 0 else "PA"
            for ci, (wop, xop, tag) in enumerate(combos):
                for dt in range(8):
                    b.step("tensor",
                           lambda e, mc=mc, dt=dt, nw=nw, wop=wop, xop=xop,
                           pb=pb, cb=cb, fi=(ci == 0 and dt == 0),
                           la=(ci == 2 and dt == 7): e.matmul(
                        pb[:, 0:R],
                        wop[:, cb + dt * nw + mc * 128:
                            cb + dt * nw + (mc + 1) * 128],
                        xop[:, dt * R:(dt + 1) * R],
                        start=fi, stop=la),
                        reads=[f"WB{tag}{gb + dt}",
                               "U2TH" if xop is U2TH else "U2TL"],
                        writes=[ppk])
            b.step("scalar", lambda e, mc=mc, dst=dst, pb=pb: e.copy(
                dst[:, mc * R:(mc + 1) * R], pb[:, 0:R]),
                reads=[ppk], writes=[dst_key])

    def proj_normal(dst, kpre, ntiles):
        combos = [(U2TH, WH, "H"), (U2TH, WL, "L"), (U2TL, WH, "H")]
        for i in range(ntiles):
            for nb in range(2):
                for ci, (xop, wop, tag) in enumerate(combos):
                    for dt in range(8):
                        b.step("tensor",
                               lambda e, i=i, nb=nb, dt=dt, xop=xop, wop=wop,
                               fi=(ci == 0 and dt == 0),
                               la=(ci == 2 and dt == 7): e.matmul(
                            PF[:, nb * 512:(nb + 1) * 512],
                            xop[:, dt * R + i * 128: dt * R + (i + 1) * 128],
                            wop[:, dt * D + nb * 512: dt * D + (nb + 1) * 512],
                            start=fi, stop=la),
                            reads=[f"WB{tag}{2 * dt + nb}",
                                   "U2TH" if xop is U2TH else "U2TL"],
                            writes=[f"PF{nb}"])
            b.step("scalar", lambda e, i=i, dst=dst: e.copy(
                dst[:, i * D:(i + 1) * D], PF[:, 0:1024]),
                reads=["PF0", "PF1"], writes=[f"{kpre}{i}"])

    def emit_gk(bt=None, kf=None):
        bts = [(GKT, mc * R) if bt is None else bt(mc) for mc in range(4)]
        gkk = [f"GKs{mc}" if kf is None else kf(mc) for mc in range(4)]
        combos = [(WGK1H, U2TH, "WGK1H"), (WGK1H, U2TL, "WGK1H"),
                  (WGK1L, U2TH, "WGK1L")]
        for ci, (wop, xop, wk) in enumerate(combos):
            for dt in range(8):
                b.step("tensor", lambda e, dt=dt, wop=wop, xop=xop,
                       fi=(ci == 0 and dt == 0),
                       la=(ci == 2 and dt == 7): e.matmul(
                    PP[0:LOW, 0:R], wop[:, dt * LOW:(dt + 1) * LOW],
                    xop[:, dt * R:(dt + 1) * R], start=fi, stop=la),
                    reads=[wk, "U2TH" if xop is U2TH else "U2TL"],
                    writes=["PP"])
        b.step("scalar", lambda e: e.copy(Z1T[0:LOW, 0:R], PP[0:LOW, 0:R]),
               reads=["PP"], writes=["Z1T"])
        for p in range(2):
            pk = "PP" if p == 0 else "PA"
            pt = PP if p == 0 else PA
            for h in range(2):
                mc = 2 * p + h
                b.step("tensor", lambda e, mc=mc, pt=pt, h=h: e.matmul(
                    pt[:, h * R:(h + 1) * R],
                    WGK2[:, mc * 128:(mc + 1) * 128], Z1T[:, 0:R],
                    start=True, stop=True),
                    reads=["WGK2", "Z1T", "Z1TONE"], writes=[pk])
            gt, gc = bts[2 * p]
            b.step("scalar", lambda e, pt=pt, gt=gt, gc=gc: e.activation(
                gt[:, gc:gc + 2 * R], pt[:, 0:2 * R], AF.Sigmoid),
                reads=[pk], writes=[gkk[2 * p], gkk[2 * p + 1]])
        for p in range(2):
            gt, gc = bts[2 * p]
            b.step("scalar", lambda e, gt=gt, gc=gc: e.activation(
                gt[:, gc:gc + 2 * R], gt[:, gc:gc + 2 * R], AF.Ln),
                reads=[gkk[2 * p], gkk[2 * p + 1]],
                writes=[gkk[2 * p], gkk[2 * p + 1]])
        for mc in range(4):
            gt, gc = bts[mc]
            for il in range(rt):
                b.step("vector", lambda e, gt=gt, gc=gc, il=il: e.tensor_tensor_scan(
                    gt[:, gc + il * 128: gc + (il + 1) * 128],
                    gt[:, gc + il * 128: gc + (il + 1) * 128],
                    gt[:, gc + il * 128: gc + (il + 1) * 128],
                    0.0, ALU.add, ALU.bypass),
                    reads=[gkk[mc]], writes=[gkk[mc]])

    def emit_scan_chunk(iloc, full, kt=None, vn=None, bt=None, kf=None,
                        psb=None, ptb=None, statb=None):
        kt_t, kt_k = kt if kt is not None else (KT, "KT")
        vn_t, vn_k = vn if vn is not None else (VN, "VN")
        bts = [(GKT, mc * R) if bt is None else bt(mc) for mc in range(4)]
        gkk = [f"GKs{mc}" if kf is None else kf(mc) for mc in range(4)]
        C_DCm, C_BHm, sfx = statb if statb is not None else (10, 14, "")
        BTT = [bts[mc][0] for mc in range(4)]
        colC = [bts[mc][1] + iloc * 128 + 127 for mc in range(4)]
        blb = [(slice(None), slice(bts[mc][1] + iloc * 128,
                                   bts[mc][1] + (iloc + 1) * 128))
               for mc in range(4)]
        slb = [(slice(None), slice(mc * R + iloc * 128,
                                   mc * R + (iloc + 1) * 128)) for mc in range(4)]
        if ptb is None:
            PTB = [PT[:, 0:128], PF[:, 512:640]]
            K_PTB = ["PT", "PF1"]
        else:
            PTB, K_PTB = ptb
        PAB = [PA[:, 0:512], PP[:, 0:512]]
        K_PAB = ["PA", "PP"]
        if psb is None:
            PSB = [PS_[:, 0:256], PF[:, 0:256]]
            K_PSB = ["PS", "PF0"]
        else:
            PSB, K_PSB = psb

        def stage(mcs, fn):
            for mc in mcs:
                fn(mc, mc % 2)

        def run_pair(mcs):
            stage(mcs, lambda mc, st_: b.step("scalar", lambda e, mc=mc: e.activation(
                ST[:, C_DCm + mc:C_DCm + mc + 1],
                BTT[mc][:, colC[mc]:colC[mc] + 1],
                AF.Exp, scale=1.0 / 16.0),
                reads=[gkk[mc]], writes=[f"dc{mc}{sfx}"]))
            stage(mcs, lambda mc, st_: b.step("vector", lambda e, mc=mc: e.tensor_scalar(
                ST[:, C_BHm + mc:C_BHm + mc + 1],
                BTT[mc][:, colC[mc]:colC[mc] + 1],
                1.0 / 16.0, None, ALU.mult),
                reads=[gkk[mc]], writes=[f"bh{mc}{sfx}"]))
            stage(mcs, lambda mc, st_: b.step("scalar", lambda e, mc=mc: e.activation(
                EXX[mc % 2][:, :], BTT[mc][blb[mc]], AF.Exp,
                bias=ST[:, C_BHm + mc:C_BHm + mc + 1], scale=-1.0 / 16.0),
                reads=[gkk[mc], f"bh{mc}{sfx}"], writes=[f"EXX{st_}"]))
            stage(mcs, lambda mc, st_: b.step("vector", lambda e, mc=mc: e.tensor_tensor(
                KHT[mc % 2][:, :], kt_t[slb[mc]], EXX[mc % 2][:, :], ALU.mult),
                reads=[kt_k, f"EXX{st_}"], writes=[f"KHT{st_}"]))
            stage(mcs, lambda mc, st_: b.step("tensor", lambda e, mc=mc: e.transpose(
                PTB[mc % 2], KHT[mc % 2][:, :], IDF[:, :]),
                reads=[f"KHT{st_}", "IDF"], writes=[K_PTB[st_]]))
            stage(mcs, lambda mc, st_: b.step("vector", lambda e, mc=mc: e.tensor_copy(
                KHN[mc % 2][:, :], PTB[mc % 2]),
                reads=[K_PTB[st_]], writes=[f"KHN{st_}"]))
            if full:
                stage(mcs, lambda mc, st_: b.step(
                    "tensor", lambda e, mc=mc, iloc=iloc: e.matmul(
                        PSB[mc % 2], KHN[mc % 2][:, :],
                        vn_t[:, iloc * D + mc * 256: iloc * D + (mc + 1) * 256],
                        start=True, stop=True),
                    reads=[f"KHN{st_}", f"{vn_k}{iloc}"], writes=[K_PSB[st_]]))

                def _eq(mc, st_):
                    b.step("scalar", lambda e, mc=mc: e.activation(
                        EXX[mc % 2][:, :], BTT[mc][blb[mc]], AF.Exp,
                        bias=CLNS[:, :], scale=1.0 / 16.0),
                        reads=[gkk[mc], "CLNS"], writes=[f"EXX{st_}"])
                    b.step("vector", lambda e, mc=mc: e.tensor_tensor(
                        QTE[mc % 2][:, :], QT[slb[mc]], EXX[mc % 2][:, :], ALU.mult),
                        reads=["QT", f"EXX{st_}"], writes=[f"QTE{st_}"])
                stage(mcs, _eq)
                def _qx(mc, st_):
                    for hh in range(4):
                        b.step("gpsimd", lambda e, mc=mc, hh=hh: e.tensor_tensor(
                            QX[mc % 2][:, hh * 128:(hh + 1) * 128], QTE[mc % 2][:, :],
                            QMASK4[:, hh * 128:(hh + 1) * 128], ALU.mult),
                            reads=[f"QTE{st_}", "QMASK4"], writes=[f"QX{st_}q{hh}"])
                stage(mcs, _qx)
                def _ek(mc, st_):
                    b.step("scalar", lambda e, mc=mc: e.activation(
                        EXX[mc % 2][:, :], BTT[mc][blb[mc]], AF.Exp,
                        scale=-1.0 / 16.0),
                        reads=[gkk[mc]], writes=[f"EXX{st_}"])
                    b.step("vector", lambda e, mc=mc: e.tensor_tensor(
                        KHT[mc % 2][:, :], kt_t[slb[mc]], EXX[mc % 2][:, :],
                        ALU.mult),
                        reads=[kt_k, f"EXX{st_}"], writes=[f"KHT{st_}"])
                stage(mcs, _ek)
                stage(mcs, lambda mc, st_: b.step("tensor", lambda e, mc=mc: e.matmul(
                    PAB[mc % 2], KHT[mc % 2][:, :], QX[mc % 2][:, 0:512],
                    start=True, stop=True),
                    reads=[f"KHT{st_}"] + [f"QX{st_}q{h}" for h in range(4)],
                    writes=[K_PAB[st_]]))
                stage(mcs, lambda mc, st_: b.step("vector", lambda e, mc=mc: e.tensor_tensor(
                    AM[mc % 2][:, :], PAB[mc % 2], MASK4[:, :], ALU.mult),
                    reads=[K_PAB[st_], "MASK4"], writes=[f"AM{st_}"]))
                def _hsx(mc, st_):
                    for hh in range(4):
                        b.step("gpsimd", lambda e, mc=mc, hh=hh: e.tensor_tensor(
                            HSX[mc % 2][:, hh * DV:(hh + 1) * DV],
                            HS[:, mc * DV:(mc + 1) * DV],
                            HMASK[:, hh * DV:(hh + 1) * DV], ALU.mult),
                            reads=[f"HS{mc}", "HMASK"], writes=[f"HSX{st_}q{hh}"])
                stage(mcs, _hsx)
                def _oacc(mc, st_):
                    pob = f"PO{mc // 2}"
                    b.step("tensor", lambda e, mc=mc: e.matmul(
                        PO[:, mc * 256:(mc + 1) * 256], QTE[mc % 2][:, :],
                        HSX[mc % 2][:, 0:256], start=True, stop=False),
                        reads=[f"QTE{st_}"] + [f"HSX{st_}q{h}" for h in range(4)],
                        writes=[pob])
                    for hh in range(4):
                        h = mc * 4 + hh
                        b.step("tensor", lambda e, mc=mc, hh=hh, h=h, iloc=iloc:
                               e.matmul(
                            PO[:, h * DV:(h + 1) * DV],
                            AM[mc % 2][:, hh * 128:(hh + 1) * 128],
                            vn_t[:, iloc * D + h * DV: iloc * D + (h + 1) * DV],
                            start=False, stop=(hh == 3)),
                            reads=[f"AM{st_}", f"{vn_k}{iloc}"], writes=[pob])
                    b.step("tensor", lambda e, mc=mc, iloc=iloc: e.matmul(
                        PSB[mc % 2], KHN[mc % 2][:, :],
                        vn_t[:, iloc * D + mc * 256: iloc * D + (mc + 1) * 256],
                        start=True, stop=True),
                        reads=[f"KHN{st_}", f"{vn_k}{iloc}"], writes=[K_PSB[st_]])
                    for hh in range(4):
                        b.step("vector", lambda e, mc=mc, hh=hh:
                               e.scalar_tensor_tensor(
                            HS[hh * 32:(hh + 1) * 32, mc * DV:(mc + 1) * DV],
                            HS[hh * 32:(hh + 1) * 32, mc * DV:(mc + 1) * DV],
                            ST[hh * 32:(hh + 1) * 32, C_DCm + mc:C_DCm + mc + 1],
                            PSB[mc % 2][hh * 32:(hh + 1) * 32,
                                        hh * DV:(hh + 1) * DV],
                            ALU.mult, ALU.add),
                            reads=[f"HS{mc}", f"dc{mc}{sfx}", K_PSB[st_]],
                            writes=[f"HS{mc}"])
                stage(mcs, _oacc)
            else:
                def _spro(mc, st_):
                    b.step("tensor", lambda e, mc=mc, iloc=iloc: e.matmul(
                        PSB[mc % 2], KHN[mc % 2][:, :],
                        vn_t[:, iloc * D + mc * 256: iloc * D + (mc + 1) * 256],
                        start=True, stop=True),
                        reads=[f"KHN{st_}", f"{vn_k}{iloc}"], writes=[K_PSB[st_]])
                    for hh in range(4):
                        b.step("vector", lambda e, mc=mc, hh=hh:
                               e.scalar_tensor_tensor(
                            HS[hh * 32:(hh + 1) * 32, mc * DV:(mc + 1) * DV],
                            HS[hh * 32:(hh + 1) * 32, mc * DV:(mc + 1) * DV],
                            ST[hh * 32:(hh + 1) * 32, C_DCm + mc:C_DCm + mc + 1],
                            PSB[mc % 2][hh * 32:(hh + 1) * 32,
                                        hh * DV:(hh + 1) * DV],
                            ALU.mult, ALU.add),
                            reads=[f"HS{mc}", f"dc{mc}{sfx}", K_PSB[st_]],
                            writes=[f"HS{mc}"])
                stage(mcs, _spro)

        run_pair((0, 1))
        run_pair((2, 3))

    def emit_o_epilogue(iloc):
        b.step("scalar", lambda e: e.copy(OSB[:, :], PO[:, 0:1024]),
               reads=["PO0", "PO1"], writes=["YSC"])
        b.step("vector", lambda e: e.tensor_tensor(
            OSQ[:, :], OSB[:, :], OSB[:, :], ALU.mult),
            reads=["YSC"], writes=["DSC"])
        b.step("vector", lambda e: e.tensor_reduce(
            MS16[:, :], OSQ[:, :].rearrange("p (h v) -> p h v", v=DV), AX.X,
            ALU.add), reads=["DSC"], writes=["MS16"])
        b.step("vector", lambda e: e.tensor_scalar(
            SC16[:, :], MS16[:, :], 1.0 / DV, EPS, ALU.mult, op1=ALU.add),
            reads=["MS16"], writes=["SC16"])
        b.step("scalar", lambda e: e.activation(MS16[:, :], SC16[:, :], AF.Ln),
               reads=["SC16"], writes=["MS16"])
        b.step("scalar", lambda e: e.activation(
            SC16[:, :], MS16[:, :], AF.Exp, scale=-0.5),
            reads=["MS16"], writes=["SC16"])
        b.step("scalar", lambda e, iloc=iloc: e.activation(
            GN[:, iloc * D:(iloc + 1) * D], GN[:, iloc * D:(iloc + 1) * D],
            AF.Silu), reads=[f"GN{iloc}"], writes=[f"GN{iloc}"])
        for hh in range(H):
            b.step("vector", lambda e, hh=hh: e.tensor_scalar(
                OSB[:, hh * DV:(hh + 1) * DV], OSB[:, hh * DV:(hh + 1) * DV],
                SC16[:, hh:hh + 1], None, ALU.mult),
                reads=["YSC", "SC16"], writes=["YSC"])
        b.step("vector", lambda e, iloc=iloc: e.tensor_tensor(
            OF2[:, iloc * D:(iloc + 1) * D], OSB[:, :],
            GN[:, iloc * D:(iloc + 1) * D], ALU.mult),
            reads=["YSC", f"GN{iloc}"], writes=[f"VN{iloc}"])

    def emit_wo(iloc, res_tile, res_key):
        for half in range(2):
            for j in range(4):
                dt = half * 4 + j
                b.step("tensor", lambda e, dt=dt, j=j, iloc=iloc: e.transpose(
                    PT[:, j * 128:(j + 1) * 128],
                    OF2[:, iloc * D + dt * 128: iloc * D + (dt + 1) * 128],
                    IDF[:, :]), reads=[f"VN{iloc}", "IDF"], writes=["PT"])
            b.step("scalar", lambda e, half=half: e.copy(
                OFTH[:, half * 512:(half + 1) * 512], PT[:, :]),
                reads=["PT"], writes=[f"OFT{half}"])
            b.step("vector", lambda e, half=half: e.tensor_tensor(
                OFTL[:, half * 512:(half + 1) * 512], PT[:, :],
                OFTH[:, half * 512:(half + 1) * 512], ALU.subtract),
                reads=["PT"], writes=[f"OFT{half}"])
        combos = [(OFTH, WH, "H"), (OFTH, WL, "L"), (OFTL, WH, "H")]
        for nb in range(2):
            for ci, (xop, wop, tag) in enumerate(combos):
                for dt in range(8):
                    b.step("tensor",
                           lambda e, nb=nb, dt=dt, xop=xop, wop=wop,
                           fi=(ci == 0 and dt == 0),
                           la=(ci == 2 and dt == 7): e.matmul(
                        PF[:, nb * 512:(nb + 1) * 512],
                        xop[:, dt * 128:(dt + 1) * 128],
                        wop[:, dt * D + nb * 512: dt * D + (nb + 1) * 512],
                        start=fi, stop=la),
                        reads=[f"OFT{dt // 4}", f"WB{tag}{2 * dt + nb}"],
                        writes=[f"PF{nb}"])
        b.step("vector", lambda e: e.tensor_tensor(
            res_tile, res_tile, PF[:, 0:1024], ALU.add),
            reads=[res_key, "PF0", "PF1"], writes=[res_key])

    b.step("gpsimd", lambda e: e.iota(PIDXI[:, :], [[0, 1]], channel_multiplier=1),
           writes=["PIDXI"])
    b.step("gpsimd", lambda e: e.iota(COLI[:, :], [[0, 4], [1, 128]],
                                      channel_multiplier=0), writes=["COLI"])
    b.step("gpsimd", lambda e: e.iota(CB32[:, 0:512], [[32, 4], [0, 128]],
                                      channel_multiplier=0), writes=["CB32"])
    b.step("gpsimd", lambda e: e.iota(CB32H[:, 0:256], [[32, 4], [0, 64]],
                                      channel_multiplier=0), writes=["CB32H"])
    b.step("gpsimd", lambda e: e.memset(HS[:, :], 0.0),
           writes=["HS0", "HS1", "HS2", "HS3"])
    b.step("gpsimd", lambda e: e.memset(CLNS[:, :], LN_S), writes=["CLNS"])
    b.step("vector", lambda e: e.tensor_copy(PIDX[:, :], PIDXI[:, :]),
           reads=["PIDXI"], writes=["PIDX"])
    b.step("vector", lambda e: e.tensor_scalar(
        MASK4[:, :], COLI[:, :], PIDX[:, :], None, ALU.is_ge),
        reads=["COLI", "PIDX"], writes=["MASK4"])
    b.step("vector", lambda e: e.tensor_scalar(
        IDF[:, :], COLI[:, 0:128], PIDX[:, :], None, ALU.is_equal),
        reads=["COLI", "PIDX"], writes=["IDF"])
    b.step("vector", lambda e: e.tensor_copy(IDFB[:, :], IDF[:, :]),
           reads=["IDF"], writes=["IDFB"])

    def _blockmask(dst, src, src_key, n, dst_key):
        b.step("vector", lambda e: e.tensor_scalar(
            YSC[:, 0:n], src, PIDX[:, :], None, ALU.subtract),
            reads=[src_key, "PIDX"], writes=["YSC"])
        b.step("vector", lambda e: e.tensor_scalar(
            DSC[:, 0:n], YSC[:, 0:n], -31.5, None, ALU.is_ge),
            reads=["YSC"], writes=["DSC"])
        b.step("vector", lambda e: e.tensor_scalar(
            T1SC[:, 0:n], YSC[:, 0:n], 0.5, None, ALU.is_le),
            reads=["YSC"], writes=["T1SC"])
        b.step("vector", lambda e: e.tensor_tensor(
            dst, DSC[:, 0:n], T1SC[:, 0:n], ALU.mult),
            reads=["DSC", "T1SC"], writes=[dst_key])

    _blockmask(QMASK4[:, :], CB32[:, 0:512], "CB32", 512, "QMASK4")
    _blockmask(HMASK[:, :], CB32H[:, 0:256], "CB32H", 256, "HMASK")

    for dt in range(8):
        b.step("sync", lambda e, dt=dt: e.dma_start(
            WGK1[:, dt * LOW:(dt + 1) * LOW], wgk1_d[dt * 128:(dt + 1) * 128, :]),
            writes=["WGK1"], dma_sem="dM")
    b.step("sync", lambda e: e.dma_start(WGK2[0:LOW, :], wgk2_d[:, :]),
           writes=["WGK2"], dma_sem="dM")
    b.step("sync", lambda e: e.dma_start(
        WGK2[LOW:LOW + 1, :], bgk2_d[:].rearrange("(o n) -> o n", o=1)),
        writes=["WGK2"], dma_sem="dM")
    b.step("gpsimd", lambda e: e.memset(Z1T[:, :], 1.0),
           writes=["Z1T", "Z1TONE"])
    for c4 in range(4):
        b.step("sync", lambda e, c4=c4: e.dma_start(
            NEGB[:, c4:c4 + 1],
            bgk2_d[c4 * 128:(c4 + 1) * 128].rearrange("(p o) -> p o", o=1)),
            writes=["NEGB"], dma_sem="dM")
    for half in range(2):
        b.step("sync", lambda e, half=half: e.dma_start(
            GNP[half * 64:(half + 1) * 64, 0:1],
            gnw_d[:].rearrange("(p o) -> p o", o=1)),
            writes=["GNP"], dma_sem="dM")
    b.step("vector", lambda e: e.tensor_copy(WGK1H[:, :], WGK1[:, :]),
           reads=["WGK1"], writes=["WGK1H"])
    b.step("vector", lambda e: e.tensor_tensor(
        WGK1L[:, :], WGK1[:, :], WGK1H[:, :], ALU.subtract),
        reads=["WGK1", "WGK1H"], writes=["WGK1L"])
    for dt in range(8):
        b.step("scalar", lambda e, dt=dt: e.dma_start(
            STG[:, :], wf_d[dt * 128:(dt + 1) * 128, :]),
            writes=["STG"], dma_sem="dXq")
        for h4 in range(2):
            for j in range(4):
                nch = h4 * 4 + j
                b.step("tensor", lambda e, nch=nch, j=j: e.transpose(
                    PT[:, j * 128:(j + 1) * 128],
                    STG[:, nch * 128:(nch + 1) * 128], IDF[:, :]),
                    reads=["STG", "IDF"], writes=["PT"])
            b.step("vector", lambda e: e.tensor_reduce(
                ST[:, 16:20], PT[:, 0:512].rearrange("p (n x) -> p n x", x=128),
                AX.X, ALU.max, apply_absolute_value=True),
                reads=["PT"], writes=["st20"])
            if dt == 0:
                b.step("vector", lambda e, h4=h4: e.tensor_copy(
                    AMX8[:, h4 * 4:(h4 + 1) * 4], ST[:, 16:20]),
                    reads=["st20"], writes=[f"AMX{h4}"])
            else:
                b.step("vector", lambda e, h4=h4: e.tensor_tensor(
                    AMX8[:, h4 * 4:(h4 + 1) * 4], AMX8[:, h4 * 4:(h4 + 1) * 4],
                    ST[:, 16:20], ALU.max),
                    reads=["st20", f"AMX{h4}"], writes=[f"AMX{h4}"])
    b.step("vector", lambda e: e.tensor_scalar(
        RS8[:, 0:8], AMX8[:, 0:8], 1e-12, 1.0 / 127.0, ALU.max, op1=ALU.mult),
        reads=[f"AMX{n}" for n in range(2)], writes=["RS8"])
    b.step("vector", lambda e: e.reciprocal(AMX8[:, 0:8], RS8[:, 0:8]),
           reads=["RS8"], writes=["AMXR"])
    b.step("vector", lambda e: e.tensor_scalar(
        RS8[:, 0:8], RS8[:, 0:8], 0.5, None, ALU.mult),
        reads=["RS8"], writes=["RS8"])
    b.step("gpsimd", lambda e: e.memset(SCD[0:1, 0:128], 1.0), writes=["SCD"])
    b.step("tensor", lambda e: e.transpose(PT[0:8, 0:128], AMX8[:, 0:8],
                                           IDF[:, :]),
           reads=["AMXR", "IDF"], writes=["PT"])
    b.step("vector", lambda e: e.tensor_copy(SCT[0:8, :], PT[0:8, 0:128]),
           reads=["PT"], writes=["SCT"])
    b.step("sync", lambda e: e.dma_start(YSC[0:1, 0:D], SCT[0:8, 0:128]),
           reads=["SCT"], writes=["YSC"], dma_sem="dS")
    b.step("tensor", lambda e: e.transpose(PT[0:8, 0:128], RS8[:, 0:8],
                                           IDF[:, :]),
           reads=["RS8", "IDF"], writes=["PT"])
    b.step("vector", lambda e: e.tensor_copy(SCT[0:8, :], PT[0:8, 0:128]),
           reads=["PT"], writes=["SCT"])
    b.step("sync", lambda e: e.dma_start(DSC[0:1, 0:D], SCT[0:8, 0:128]),
           reads=["SCT"], writes=["DSC"], dma_sem="dS")
    for nb in range(2):
        b.step("tensor", lambda e, nb=nb: e.matmul(
            PF[:, nb * 512:(nb + 1) * 512], SCD[0:1, 0:128],
            YSC[0:1, nb * 512:(nb + 1) * 512], start=True, stop=True),
            reads=["SCD", "YSC"], writes=[f"PF{nb}"])
    b.step("vector", lambda e: e.tensor_copy(RSCB[:, :], PF[:, 0:1024]),
           reads=["PF0", "PF1"], writes=["RSCB"])
    for nb in range(2):
        b.step("tensor", lambda e, nb=nb: e.matmul(
            PF[:, nb * 512:(nb + 1) * 512], SCD[0:1, 0:128],
            DSC[0:1, nb * 512:(nb + 1) * 512], start=True, stop=True),
            reads=["SCD", "DSC"], writes=[f"PF{nb}"])
    b.step("vector", lambda e: e.tensor_copy(ABC[:, :], PF[:, 0:1024]),
           reads=["PF0", "PF1"], writes=["ABC"])
    for dt in range(8):
        b.step("gpsimd", lambda e, dt=dt: e.dma_start(
            STG[:, :], wf_d[dt * 128:(dt + 1) * 128, :]),
            writes=["STG"], dma_sem="dW2")
        b.step("vector", lambda e: e.tensor_tensor(
            YSC[:, :], STG[:, :], RSCB[:, :], ALU.mult),
            reads=["STG", "RSCB"], writes=["YSC"])
        b.step("vector", lambda e: e.tensor_copy(ISC[:, :], YSC[:, :]),
               reads=["YSC"], writes=["ISC"])
        b.step("vector", lambda e: e.tensor_tensor(
            DSC[:, :], YSC[:, :], ISC[:, :], ALU.subtract),
            reads=["YSC", "ISC"], writes=["DSC"])
        b.step("vector", lambda e: e.tensor_scalar(
            C2SC[:, :], DSC[:, :], -0.5, None, ALU.is_lt),
            reads=["DSC"], writes=["C2SC"])
        b.step("vector", lambda e: e.scalar_tensor_tensor(
            T1SC[:, :], DSC[:, :], 0.5, ISC[:, :], ALU.is_gt, ALU.add),
            reads=["DSC", "ISC"], writes=["T1SC"])
        b.step("vector", lambda e, dt=dt: e.tensor_tensor(
            QFFN[:, dt * D:(dt + 1) * D], T1SC[:, :], C2SC[:, :], ALU.subtract),
            reads=["T1SC", "C2SC"], writes=["QFFN"])
    for i in range(ntm - 2 * rt):
        b.step("scalar", lambda e, i=i: e.dma_start(
            XR[:, i * D:(i + 1) * D], x_d[(ntp + i) * 128:(ntp + i + 1) * 128, :]),
            writes=[f"XR{i}"], dma_sem="dX")
    for nm, dram, ncol in (("wk", wk_d, KD), ("wv", wv_d, D)):
        hi_d, lo_d = wsp[nm]
        for dt in range(8):
            b.step("sync", lambda e, dt=dt, dram=dram, ncol=ncol: e.dma_start(
                STG[:, 0:ncol], dram[dt * 128:(dt + 1) * 128, :]),
                writes=["STG"], dma_sem="dW")
            b.step("scalar", lambda e, ncol=ncol: e.copy(
                T1SC[:, 0:ncol], STG[:, 0:ncol]),
                reads=["STG"], writes=["T1SC"])
            b.step("vector", lambda e, ncol=ncol: e.tensor_tensor(
                C2SC[:, 0:ncol], STG[:, 0:ncol], T1SC[:, 0:ncol], ALU.subtract),
                reads=["STG", "T1SC"], writes=["C2SC"])
            b.step("sync", lambda e, dt=dt, hi_d=hi_d, ncol=ncol: e.dma_start(
                hi_d[dt * 128:(dt + 1) * 128, :], T1SC[:, 0:ncol]),
                reads=["T1SC"], writes=[f"{nm}h{dt}"], dma_sem="dS")
            b.step("sync", lambda e, dt=dt, lo_d=lo_d, ncol=ncol: e.dma_start(
                lo_d[dt * 128:(dt + 1) * 128, :], C2SC[:, 0:ncol]),
                reads=["C2SC"], writes=[f"{nm}l{dt}"], dma_sem="dS")
    b.step("sync", lambda e: e.nop(), waits=[("dS", sems["dS"].n)])
    b.flush("setup")
    setup_ctx.close()

    n_pro_rounds = ntp // rt
    n_main_rounds = ntm // rt

    for nm, dram, ncol in (("wq", wq_d, KD), ("wg", wg_d, D), ("wo", wo_d, D)):
        hi_d, lo_d = wsp[nm]
        for dt in range(8):
            b.step("gpsimd", lambda e, dt=dt, dram=dram, ncol=ncol: e.dma_start(
                SCD[:, 0:ncol], dram[dt * 128:(dt + 1) * 128, :]),
                writes=["SCD"], dma_sem="dW2")
            if nm == "wo":
                b.step("gpsimd", lambda e, ncol=ncol: e.tensor_scalar(
                    SCD[:, 0:ncol], SCD[:, 0:ncol], GNP[:, 0:1], None,
                    ALU.mult), reads=["SCD", "GNP"], writes=["SCD"])
            b.step("gpsimd", lambda e, ncol=ncol: e.tensor_copy(
                OFTH[:, 0:ncol], SCD[:, 0:ncol]),
                reads=["SCD"], writes=["OFT0", "OFT1"])
            b.step("gpsimd", lambda e, ncol=ncol: e.tensor_tensor(
                OFTL[:, 0:ncol], SCD[:, 0:ncol], OFTH[:, 0:ncol],
                ALU.subtract),
                reads=["SCD", "OFT0", "OFT1"], writes=["OFT0", "OFT1"])
            b.step("gpsimd", lambda e, dt=dt, hi_d=hi_d, ncol=ncol: e.dma_start(
                hi_d[dt * 128:(dt + 1) * 128, :], OFTH[:, 0:ncol]),
                reads=["OFT0", "OFT1"], writes=[f"{nm}h{dt}"], dma_sem="dS2")
            b.step("gpsimd", lambda e, dt=dt, lo_d=lo_d, ncol=ncol: e.dma_start(
                lo_d[dt * 128:(dt + 1) * 128, :], OFTL[:, 0:ncol]),
                reads=["OFT0", "OFT1"], writes=[f"{nm}l{dt}"], dma_sem="dS2")

    quanted = set()
    applied = set()

    def apply_b(nxt):
        if nxt >= n_main_rounds or nxt not in quanted or nxt in applied:
            return
        applied.add(nxt)
        for il in range(rt):
            t = nxt * rt + il
            emit_ffn_apply(XR[:, t * D:(t + 1) * D], f"XR{t}", FQB(il),
                           slc=(18 + il, f"stq{il}"), altp=True)
    dle_rnd = n_pro_rounds + min(2, (ntm - 2 * rt) // rt)

    def quant_next(nxt, at_rnd):
        if nxt >= n_main_rounds or nxt in quanted:
            return
        for il in range(rt):
            t = nxt * rt + il
            if t >= ntm - 2 * rt and at_rnd < dle_rnd:
                return
        quanted.add(nxt)
        for il in range(rt):
            t = nxt * rt + il
            emit_ffn_quant(XR[:, t * D:(t + 1) * D], f"XR{t}", FQB(il),
                           slc=(18 + il, f"stq{il}"))
    for rnd in range(n_pro_rounds + n_main_rounds):
        pro = rnd < n_pro_rounds
        if pro:
            base = ntm - 2 * rt + (rnd % 2) * rt
            res = [(XR[:, (base + il) * D:(base + il + 1) * D],
                    f"XR{base + il}") for il in range(rt)]
            for il in range(rt):
                gi = rnd * rt + il
                b.step("scalar", lambda e, il=il, gi=gi, base=base: e.dma_start(
                    XR[:, (base + il) * D:(base + il + 1) * D],
                    x_d[gi * 128:(gi + 1) * 128, :]),
                    writes=[f"XR{base + il}"], dma_sem="dX")
        else:
            mrnd = rnd - n_pro_rounds
            if rnd == n_pro_rounds + min(2, (ntm - 2 * rt) // rt):
                for i in range(ntm - 2 * rt, ntm):
                    b.step("scalar", lambda e, i=i: e.dma_start(
                        XR[:, i * D:(i + 1) * D],
                        x_d[(ntp + i) * 128:(ntp + i + 1) * 128, :]),
                        writes=[f"XR{i}"], dma_sem="dX2")
            res = [(XR[:, (mrnd * rt + il) * D:(mrnd * rt + il + 1) * D],
                    f"XR{mrnd * rt + il}") for il in range(rt)]

        if pro:
            for tile, key in res:
                emit_ffn(tile, key)
        else:
            quant_next(rnd - n_pro_rounds, rnd)
            if (rnd - n_pro_rounds) not in applied:
                applied.add(rnd - n_pro_rounds)
                for il, (tile, key) in enumerate(res):
                    emit_ffn_apply(tile, key, FQB(il),
                                   slc=(18 + il, f"stq{il}"), altp=True)
        for il, (tile, key) in enumerate(res):
            emit_ln2_u2t(tile, key, il)
        if pro and rnd % 2 == 1:
            ktp, vnp = (QT, "QT"), (GN, "GN")
            btp = lambda mc: (AM[mc // 2], (mc % 2) * R)
            kfp = lambda mc: f"GKb{mc}"
            stp = (34, 38, "b")
        else:
            ktp, vnp, btp, kfp, stp = (KT, "KT"), (VN, "VN"), None, None, None
        if pro:
            psbp = ([PO[:, 0:256], PO[:, 512:768]], ["PO0", "PO1"])
            ptbp = ([PO[:, 256:384], PO[:, 768:896]], ["PO0", "PO1"])
        else:
            psbp = ptbp = None
        emit_gk(btp, kfp)
        if not pro:
            load_weight("wq", KD)
            load_weight("wk", KD, half=1)
            proj_transposed(QT, "QT", KD)
            proj_transposed(ktp[0], ktp[1], KD, half=1)
            quant_next(rnd - n_pro_rounds + 1, rnd)
        else:
            load_weight("wk", KD, pro=pro)
            proj_transposed(ktp[0], ktp[1], KD)
        load_weight("wv", D, pro=pro)
        proj_normal(vnp[0], vnp[1], rt)
        if not pro:
            load_weight("wg", D)
            proj_normal(GN, "GN", rt)
        if not pro:
            load_weight("wo", D)
        for il in range(rt):
            emit_scan_chunk(il, full=(not pro), kt=ktp, vn=vnp, bt=btp,
                            kf=kfp, psb=psbp, ptb=ptbp, statb=stp)
            if not pro:
                emit_o_epilogue(il)
        if pro and rnd == n_pro_rounds - 1:
            quant_next(0, rnd)
        if not pro:
            quant_next(rnd - n_pro_rounds + 1, rnd)
            mrnd = rnd - n_pro_rounds
            emit_wo(0, res[0][0], res[0][1])
            SB_[0] = 24
            emit_ffn_quant(res[0][0], res[0][1], slc=(20, "stq20"))
            SB_[0] = 0
            emit_wo(1, res[1][0], res[1][1])
            SB_[0] = 24
            emit_ffn_apply(res[0][0], res[0][1], slc=(20, "stq20"))
            b.step("sync", lambda e, gi=mrnd * rt, tile=res[0][0]: e.dma_start(
                y_d[gi * 128:(gi + 1) * 128, :], tile),
                reads=[res[0][1]], writes=[f"Y{mrnd * rt}"], dma_sem="dO")
            emit_ffn_quant(res[1][0], res[1][1], slc=(21, "stq21"))
            SB_[0] = 0
            apply_b(mrnd + 1)
            SB_[0] = 24
            emit_ffn_apply(res[1][0], res[1][1], slc=(21, "stq21"))
            SB_[0] = 0
            b.step("sync", lambda e, gi=mrnd * rt + 1, tile=res[1][0]:
                   e.dma_start(
                y_d[gi * 128:(gi + 1) * 128, :], tile),
                reads=[res[1][1]], writes=[f"Y{mrnd * rt + 1}"], dma_sem="dO")
    b.step("sync", lambda e: e.nop(), waits=[("dO", 16 * ntm)])
    b.flush("rounds")
    ctx.close()
    return nc


def kernel(L0_attn_x, L0_attn_Wq, L0_attn_Wk, L0_attn_Wv, L0_attn_Wgk1,
           L0_attn_Wgk2, L0_attn_bgk2, L0_attn_Wg, L0_attn_gnw, L0_attn_Wo,
           L0_ffn_rhs):
    NTM, NTP = 16, 6
    TCK = NTM * 128
    W = NTP * 128
    x = np.ascontiguousarray(np.asarray(L0_attn_x, dtype=np.float32).reshape(-1, D))
    T = x.shape[0]
    xp = np.concatenate([np.zeros((W, D), np.float32), x], axis=0)
    nc = build_program(ntm=NTM, ntp=NTP, rt=2)
    ws = {
        "wq": L0_attn_Wq, "wk": L0_attn_Wk, "wv": L0_attn_Wv,
        "wgk1": L0_attn_Wgk1, "wgk2": L0_attn_Wgk2, "bgk2": L0_attn_bgk2,
        "wg": L0_attn_Wg, "gnw": L0_attn_gnw, "wo": L0_attn_Wo, "wf": L0_ffn_rhs,
    }
    ws = {k: np.ascontiguousarray(np.asarray(v, dtype=np.float32))
          for k, v in ws.items()}
    in_maps = []
    for c in range(8):
        sl = xp[c * TCK: c * TCK + TCK + W]
        in_maps.append({"x": np.ascontiguousarray(sl), **ws})
    res = run_bass_kernel_spmd(nc, in_maps, core_ids=list(range(8)))
    out = np.concatenate([res.results[c]["y"] for c in range(8)], axis=0)
    return out.reshape(1, T, D)


# version 46 (final)
# speedup vs baseline: 1.0091x; 1.0091x over previous
import math
import contextlib
import numpy as np

import concourse.bass as bass
import concourse.mybir as mybir
from concourse.bass_utils import run_bass_kernel_spmd

F32 = mybir.dt.float32
BF16 = mybir.dt.bfloat16
I32 = mybir.dt.int32
AF = mybir.ActivationFunctionType
ALU = mybir.AluOpType
AX = mybir.AxisListType

D = 1024
H = 16
DK = 32
DV = 64
KD = 512
LOW = 16
EPS = 1e-5
LN_S = math.log(DK ** -0.5)


class Sem:
    def __init__(self, h):
        self.h = h
        self.n = 0

    def bump(self, k=1):
        self.n += k
        return self.n


class Step:
    __slots__ = ("eng", "emit", "incs", "needs_inc", "token", "drain",
                 "waits_sym", "waits_extra", "ord", "block", "compl_dma")

    def __init__(self, eng, emit, incs, ordinal, block):
        self.eng = eng
        self.emit = emit
        self.incs = incs
        self.needs_inc = False
        self.token = None
        self.drain = False
        self.waits_sym = []
        self.waits_extra = []
        self.ord = ordinal
        self.block = block
        self.compl_dma = None


class Builder2:

    ENGINES = ("tensor", "vector", "scalar", "gpsimd", "sync")
    ENG_SEM = {"tensor": "sT", "vector": "sV", "scalar": "sS",
               "gpsimd": "sG", "sync": "sY"}

    def __init__(self, nc, sems):
        self.nc = nc
        self.sems = sems
        self.key = {}
        self.ordn = 0
        self.block_idx = 0
        self.streams = {e: [] for e in self.ENGINES}
        self.eng_drained_upto = {e: -1 for e in self.ENGINES}

    def step(self, eng, emit, reads=(), writes=(), incs=(), waits=(),
             dma_sem=None):
        il = []
        for sname, k in incs:
            s = self.sems[sname]
            s.bump(k)
            il.append((s.h, k, s.n))
        st = Step(eng, emit, il, self.ordn, self.block_idx)
        self.ordn += 1
        st.waits_extra = list(waits)

        deps = []
        need_drain = False
        for k in reads:
            ks = self.key.get(k)
            if ks is not None and ks["w"] is not None:
                deps.append(ks["w"])
        for k in writes:
            ks = self.key.get(k)
            if ks is not None:
                if ks["r"]:
                    deps.extend(ks["r"])
                elif ks["w"] is not None:
                    deps.append(ks["w"])
        for d in deps:
            if d.compl_dma is not None:
                sn, val = d.compl_dma
                st.waits_extra.append((sn, max(val, self.sems[sn].n)))
            elif d.eng == eng:
                if eng != "tensor" and d.ord > self.eng_drained_upto[eng]:
                    need_drain = True
            else:
                st.waits_sym.append(d)
        if need_drain:
            st.drain = True
            self.eng_drained_upto[eng] = st.ord - 1
        if dma_sem is not None:
            sm = self.sems[dma_sem]
            sm.bump(16)
            st.compl_dma = (dma_sem, sm.n)
        for k in reads:
            ks = self.key.setdefault(k, {"w": None, "r": []})
            ks["r"].append(st)
        for k in writes:
            self.key[k] = {"w": st, "r": []}
        self.streams[eng].append(st)
        return st

    def flush(self, name=None):
        nc = self.nc
        for eng in self.ENGINES:
            for st in self.streams[eng]:
                for d in st.waits_sym:
                    if d.compl_dma is None and not d.needs_inc:
                        if d.block == self.block_idx:
                            d.needs_inc = True
        for eng in self.ENGINES:
            s = self.sems[self.ENG_SEM[eng]]
            for st in self.streams[eng]:
                if st.needs_inc:
                    s.bump(1)
                    st.token = s.n
        with nc.Block(name) as block:
            for eng in self.ENGINES:
                ops = self.streams[eng]
                if not ops:
                    continue

                def body(e, ops=ops, eng=eng):
                    hw = {}
                    for st in ops:
                        if st.drain:
                            e.drain()
                        wl = {}
                        for d in st.waits_sym:
                            if d.block < st.block:
                                continue
                            sn, val = self.ENG_SEM[d.eng], d.token
                            assert val is not None
                            if wl.get(sn, 0) < val:
                                wl[sn] = val
                        for sn, val in st.waits_extra:
                            if wl.get(sn, 0) < val:
                                wl[sn] = val
                        for sn, val in wl.items():
                            if hw.get(sn, 0) < val:
                                hw[sn] = val
                                e.wait_ge(self.sems[sn].h, val)
                        instr = st.emit(e)
                        if st.compl_dma is not None:
                            assert instr is not None
                            instr = instr.then_inc(
                                self.sems[st.compl_dma[0]].h, 16)
                        if st.needs_inc:
                            assert instr is not None
                            instr = instr.then_inc(
                                self.sems[self.ENG_SEM[eng]].h, 1)
                        for h, k, _ in st.incs:
                            assert instr is not None
                            instr = instr.then_inc(h, k)

                getattr(block, eng)(body)
        self.streams = {e: [] for e in self.ENGINES}
        self.block_idx += 1
        for e in self.ENGINES:
            self.eng_drained_upto[e] = self.ordn - 1


def build_program(ntm=16, ntp=6, rt=2):
    assert ntm % rt == 0 and ntp % rt == 0
    R = rt * 128
    nc = bass.Bass("TRN2", target_bir_lowering=False)

    rows_in = (ntm + ntp) * 128
    x_d = nc.dram_tensor("x", [rows_in, D], F32, kind="ExternalInput")
    y_d = nc.dram_tensor("y", [ntm * 128, D], F32, kind="ExternalOutput")
    wq_d = nc.dram_tensor("wq", [D, KD], F32, kind="ExternalInput")
    wk_d = nc.dram_tensor("wk", [D, KD], F32, kind="ExternalInput")
    wv_d = nc.dram_tensor("wv", [D, D], F32, kind="ExternalInput")
    wgk1_d = nc.dram_tensor("wgk1", [D, LOW], F32, kind="ExternalInput")
    wgk2_d = nc.dram_tensor("wgk2", [LOW, KD], F32, kind="ExternalInput")
    bgk2_d = nc.dram_tensor("bgk2", [KD], F32, kind="ExternalInput")
    wg_d = nc.dram_tensor("wg", [D, D], F32, kind="ExternalInput")
    gnw_d = nc.dram_tensor("gnw", [DV], F32, kind="ExternalInput")
    wo_d = nc.dram_tensor("wo", [D, D], F32, kind="ExternalInput")
    wf_d = nc.dram_tensor("wf", [D, D], F32, kind="ExternalInput")
    wsp = {}
    for nm, ncol in (("wq", KD), ("wk", KD), ("wv", D), ("wg", D), ("wo", D)):
        wsp[nm] = (nc.dram_tensor(nm + "h", [D, ncol], BF16),
                   nc.dram_tensor(nm + "l", [D, ncol], BF16))

    ctx = contextlib.ExitStack()
    sb = lambda name, shape, dt: ctx.enter_context(nc.sbuf_tensor(name, shape, dt))
    ps = lambda name, shape, dt: ctx.enter_context(nc.psum_tensor(name, shape, dt))
    sems = {}
    for sname in ("sT", "sV", "sS", "sG", "sY", "dW", "dW2", "dW3", "dX", "dX2",
                  "dXq", "dO", "dM", "dS", "dS2"):
        sems[sname] = Sem(ctx.enter_context(nc.semaphore(sname)))

    XR = sb("XR", [128, ntm * D], F32)
    WH = sb("WH", [128, 8 * D], BF16)
    WL = sb("WL", [128, 8 * D], BF16)

    QFFN = sb("QFFN", [128, 8 * D], BF16)
    WGK1H = sb("WGK1H", [128, 8 * LOW], BF16)
    WGK1L = sb("WGK1L", [128, 8 * LOW], BF16)
    WGK2 = sb("WGK2", [LOW + 1, KD], F32)
    NEGB = sb("NEGB", [128, 4], F32)
    GNP = sb("GNP", [128, 1], F32)
    IDF = sb("IDF", [128, 128], F32)
    MASK4 = sb("MASK4", [128, 512], BF16)
    ABC = sb("ABC", [128, D], F32)
    HS = sb("HS", [128, 4 * DV], F32)
    U2TH = sb("U2TH", [128, 8 * R], BF16)
    U2TL = sb("U2TL", [128, 8 * R], BF16)
    QT = sb("QT", [128, 4 * R], F32)
    KT = sb("KT", [128, 4 * R], F32)
    GKT = sb("GKT", [128, 4 * R], F32)
    BT = GKT
    Z1T = sb("Z1T", [LOW + 1, R], F32)
    VN = sb("VN", [128, rt * D], F32)
    GN = sb("GN", [128, rt * D], F32)
    OF2 = VN
    YSC = sb("YSC", [128, D], F32)
    DSC = sb("DSC", [128, D], F32)
    ISC = sb("ISC", [128, D], mybir.dt.int16)
    C2SC = sb("C2SC", [128, D], BF16)
    T1SC = sb("T1SC", [128, D], BF16)
    T1B2 = sb("T1B2", [128, 2 * D], BF16)
    IDFB = sb("IDFB", [128, 128], BF16)
    UQT = sb("UQT", [128, 8 * 128], BF16)
    SCD = sb("SCD", [128, D], F32)
    OSB = YSC
    OSQ = DSC
    OFTH = sb("OFTH", [128, 8 * 128], BF16)
    OFTL = sb("OFTL", [128, 8 * 128], BF16)
    EXX = [sb("EXX0", [128, 128], F32), sb("EXX1", [128, 128], F32)]
    QTE = [sb("QTE0", [128, 128], F32), sb("QTE1", [128, 128], F32)]
    KHT = [sb("KHT0", [128, 128], F32), sb("KHT1", [128, 128], F32)]
    KHN = [sb("KHN0", [128, 128], F32), sb("KHN1", [128, 128], F32)]
    QX = [sb("QX0", [128, 512], F32), sb("QX1", [128, 512], F32)]
    HSX = [sb("HSX0", [128, 256], F32), sb("HSX1", [128, 256], F32)]
    AM = [sb("AM0", [128, 512], F32), sb("AM1", [128, 512], F32)]
    QMASK4 = sb("QMASK4", [128, 512], BF16)
    HMASK = sb("HMASK", [128, 256], BF16)
    ST = sb("ST", [128, 48], F32)
    BN = sb("BN", [128, 16], F32)
    MS16 = sb("MS16", [128, 16], F32)
    SC16 = sb("SC16", [128, 16], F32)
    SCT = sb("SCT", [LOW, 128], F32)
    CLNS = sb("CLNS", [128, 1], F32)
    setup_ctx = contextlib.ExitStack()
    ssb = lambda name, shape, dt: setup_ctx.enter_context(
        nc.sbuf_tensor(name, shape, dt))
    PIDXI = ssb("PIDXI", [128, 1], I32)
    PIDX = ssb("PIDX", [128, 1], F32)
    COLI = ssb("COLI", [128, 512], I32)
    AMX8 = ssb("AMX8", [128, 8], F32)
    RS8 = ssb("RS8", [128, 8], F32)
    CB32 = ssb("CB32", [128, 512], I32)
    CB32H = ssb("CB32H", [128, 256], I32)


    PT = ps("PT", [128, 512], F32)
    PA = ps("PA", [128, 512], F32)
    PO = ps("PO", [128, 1024], F32)
    PS_ = ps("PS_", [128, 256], F32)
    PP = ps("PP", [128, 512], F32)
    PF = ps("PF", [128, 1024], F32)

    STG = VN[:, 0:D]
    RSCB = GN[:, 0:D]
    WGK1 = GN[:, D:D + 8 * LOW]
    b = Builder2(nc, sems)
    C_MEAN, C_ISTD, C_AMX, C_SL, C_AA, C_BB, C_T0, C_T1, C_BH, C_DC = range(10)
    SB_ = [0]

    def stat(c):
        return ST[:, SB_[0] + c:SB_[0] + c + 1]

    def K_STf(i):
        return f"st{SB_[0] + i}"

    class _KST:
        def __getitem__(self, i):
            return K_STf(i)

    K_ST = _KST()

    def rsqrt_stat(src_c, dst_c, kin, kout):
        b.step("scalar", lambda e: e.activation(stat(C_T1), stat(src_c), AF.Ln),
               reads=[kin], writes=[K_ST[C_T1]])
        b.step("scalar", lambda e: e.activation(
            stat(dst_c), stat(C_T1), AF.Exp, scale=-0.5),
            reads=[K_ST[C_T1]], writes=[kout])

    def ln_stats(src_tile, src_keys, need_amax=True, scr=None):
        sc_t, sc_k = (DSC, "DSC") if scr is None else scr
        b.step("vector", lambda e: e.bn_stats(BN[:, 0:6], src_tile[:, 0:512]),
               reads=src_keys, writes=["bn.a"])
        b.step("vector", lambda e: e.bn_stats(BN[:, 6:12], src_tile[:, 512:1024]),
               reads=src_keys, writes=["bn.b"])
        b.step("vector", lambda e: e.bn_aggr(BN[:, 12:14], BN[:, 0:12]),
               reads=["bn.a", "bn.b"], writes=["bn.m"])
        b.step("vector", lambda e: e.tensor_copy(stat(C_MEAN), BN[:, 12:13]),
               reads=["bn.m"], writes=[K_ST[C_MEAN]])
        if need_amax:
            b.step("vector", lambda e: e.tensor_scalar(
                sc_t[:, :], src_tile, stat(C_MEAN), None, ALU.subtract),
                reads=src_keys + [K_ST[C_MEAN]], writes=[sc_k])
            b.step("vector", lambda e: e.tensor_reduce(
                stat(C_AMX), sc_t[:, :], AX.X, ALU.max,
                apply_absolute_value=True),
                reads=[sc_k], writes=[K_ST[C_AMX]])
        b.step("vector", lambda e: e.tensor_scalar(
            stat(C_T0), BN[:, 13:14], EPS, None, ALU.add),
            reads=["bn.m"], writes=[K_ST[C_T0]])
        rsqrt_stat(C_T0, C_ISTD, K_ST[C_T0], K_ST[C_ISTD])

    FQA = (YSC, "YSC", ISC, "ISC", C2SC, "C2SC", T1SC, 0, "T1SC")

    def FQB(il):
        return (SCD, "SCD", ISC, "ISC", C2SC, "C2SC",
                T1B2, il * D, f"T1B{il}")

    def emit_ffn_quant(res_tile, res_key, S=None, slc=None):
        Y, YK, I_, IK, C2, C2K, T1, T1o, T1K = S if S is not None else FQA
        slcol, slkey = (SB_[0] + C_SL, K_ST[C_SL]) if slc is None else slc
        ln_stats(res_tile, [res_key], scr=(Y, YK))
        b.step("vector", lambda e: e.tensor_tensor(
            stat(C_T0), stat(C_AMX), stat(C_ISTD), ALU.mult),
            reads=[K_ST[C_AMX], K_ST[C_ISTD]], writes=[K_ST[C_T0]])
        b.step("vector", lambda e, slcol=slcol: e.tensor_scalar(
            ST[:, slcol:slcol + 1], stat(C_T0), 1e-12, 1.0 / 127.0, ALU.max,
            op1=ALU.mult),
            reads=[K_ST[C_T0]], writes=[slkey])
        b.step("vector", lambda e, slcol=slcol: e.reciprocal(
            stat(C_T0), ST[:, slcol:slcol + 1]),
            reads=[slkey], writes=[K_ST[C_T0]])
        b.step("vector", lambda e: e.tensor_tensor(
            stat(C_AA), stat(C_ISTD), stat(C_T0), ALU.mult),
            reads=[K_ST[C_ISTD], K_ST[C_T0]], writes=[K_ST[C_AA]])
        b.step("vector", lambda e: e.tensor_tensor(
            stat(C_T1), stat(C_MEAN), stat(C_AA), ALU.mult),
            reads=[K_ST[C_MEAN], K_ST[C_AA]], writes=[K_ST[C_T1]])
        b.step("vector", lambda e: e.tensor_scalar(
            stat(C_BB), stat(C_T1), -1.0, None, ALU.mult),
            reads=[K_ST[C_T1]], writes=[K_ST[C_BB]])
        b.step("scalar", lambda e, Y=Y: e.activation(
            Y[:, :], res_tile, AF.Identity, bias=stat(C_BB), scale=stat(C_AA)),
            reads=[res_key, K_ST[C_BB], K_ST[C_AA]], writes=[YK])
        b.step("vector", lambda e, Y=Y, I_=I_: e.tensor_copy(I_[:, :], Y[:, :]),
               reads=[YK], writes=[IK])
        b.step("vector", lambda e, Y=Y, I_=I_, C2=C2: e.scalar_tensor_tensor(
            C2[:, :], Y[:, :], 0.5, I_[:, :], ALU.add, ALU.is_lt),
            reads=[YK, IK], writes=[C2K])
        b.step("vector", lambda e, Y=Y, I_=I_, T1=T1, T1o=T1o:
               e.scalar_tensor_tensor(
            T1[:, T1o:T1o + D], Y[:, :], -0.5, I_[:, :], ALU.add, ALU.is_gt),
            reads=[YK, IK], writes=[T1K])
        b.step("vector", lambda e, I_=I_, T1=T1, T1o=T1o: e.tensor_tensor(
            T1[:, T1o:T1o + D], I_[:, :], T1[:, T1o:T1o + D], ALU.add),
            reads=[IK, T1K], writes=[T1K])
        b.step("vector", lambda e, T1=T1, C2=C2, T1o=T1o: e.tensor_tensor(
            T1[:, T1o:T1o + D], T1[:, T1o:T1o + D], C2[:, :], ALU.subtract),
            reads=[T1K, C2K], writes=[T1K])

    def emit_ffn_apply(res_tile, res_key, S=None, slc=None, altp=False):
        Y, YK, I_, IK, C2, C2K, T1, T1o, T1K = S if S is not None else FQA
        slcol, slkey = (SB_[0] + C_SL, K_ST[C_SL]) if slc is None else slc
        tpb, tpk = (PA, "PA") if altp else (PT, "PT")
        mmb, mmk = (PO, ["PO0", "PO1"]) if altp else (PF, ["PF0", "PF1"])
        uq = OFTH if altp else UQT
        uqk = ["OFT0", "OFT1"] if altp else ["UQT0", "UQT1"]
        for half in range(2):
            for j in range(4):
                dt = half * 4 + j
                b.step("tensor", lambda e, dt=dt, j=j, T1=T1, T1o=T1o:
                       e.matmul(
                    tpb[:, j * 128:(j + 1) * 128],
                    T1[:, T1o + dt * 128:T1o + (dt + 1) * 128],
                    IDFB[:, :], start=True, stop=True),
                    reads=[T1K, "IDFB"], writes=[tpk])
            b.step("scalar", lambda e, half=half, uq=uq: e.copy(
                uq[:, half * 512:(half + 1) * 512], tpb[:, 0:512]),
                reads=[tpk], writes=[uqk[half]])
        for nb in range(2):
            for dt in range(8):
                b.step("tensor", lambda e, nb=nb, dt=dt, uq=uq: e.matmul(
                    mmb[:, nb * 512:(nb + 1) * 512],
                    uq[:, dt * 128:(dt + 1) * 128],
                    QFFN[:, dt * D + nb * 512: dt * D + (nb + 1) * 512],
                    start=(dt == 0), stop=(dt == 7)),
                    reads=[uqk[dt // 4], "QFFN"], writes=[mmk[nb]])
        b.step("vector", lambda e: e.tensor_tensor(
            DSC[:, :], mmb[:, 0:1024], ABC[:, :], ALU.mult),
            reads=mmk + ["ABC"], writes=["DSC"])
        b.step("vector", lambda e, slcol=slcol: e.scalar_tensor_tensor(
            res_tile, DSC[:, :], ST[:, slcol:slcol + 1], res_tile,
            ALU.mult, ALU.add),
            reads=["DSC", slkey, res_key], writes=[res_key])

    def emit_ffn(res_tile, res_key, S=None):
        emit_ffn_quant(res_tile, res_key, S)
        emit_ffn_apply(res_tile, res_key, S)

    def emit_ln2_u2t(res_tile, res_key, iloc):
        ln_stats(res_tile, [res_key], need_amax=False)
        b.step("vector", lambda e: e.tensor_tensor(
            stat(C_T0), stat(C_MEAN), stat(C_ISTD), ALU.mult),
            reads=[K_ST[C_MEAN], K_ST[C_ISTD]], writes=[K_ST[C_T0]])
        b.step("vector", lambda e: e.tensor_scalar(
            stat(C_BB), stat(C_T0), -1.0, None, ALU.mult),
            reads=[K_ST[C_T0]], writes=[K_ST[C_BB]])
        b.step("scalar", lambda e: e.activation(
            YSC[:, :], res_tile, AF.Identity, bias=stat(C_BB), scale=stat(C_ISTD)),
            reads=[res_key, K_ST[C_BB], K_ST[C_ISTD]], writes=["YSC"])
        for half in range(2):
            for j in range(4):
                dt = half * 4 + j
                b.step("tensor", lambda e, dt=dt, j=j: e.transpose(
                    PT[:, j * 128:(j + 1) * 128], YSC[:, dt * 128:(dt + 1) * 128],
                    IDF[:, :]), reads=["YSC", "IDF"], writes=["PT"])
            for j in range(4):
                dt = half * 4 + j
                sl = slice(dt * R + iloc * 128, dt * R + (iloc + 1) * 128)
                b.step("scalar", lambda e, j=j, sl=sl: e.copy(
                    U2TH[:, sl], PT[:, j * 128:(j + 1) * 128]),
                    reads=["PT"], writes=["U2TH"])
                b.step("vector", lambda e, j=j, sl=sl: e.tensor_tensor(
                    U2TL[:, sl], PT[:, j * 128:(j + 1) * 128], U2TH[:, sl],
                    ALU.subtract),
                    reads=["PT", "U2TH"], writes=["U2TL"])

    WQMAP = {"wq": (("sync", "dW"), ("gpsimd", "dW2")),
             "wk": (("sync", "dW"), ("gpsimd", "dW2")),
             "wv": (("sync", "dW"), ("gpsimd", "dW2")),
             "wg": (("sync", "dW"), ("gpsimd", "dW2")),
             "wo": (("sync", "dW"), ("gpsimd", "dW2"))}

    def load_weight(nm, nw, pro=False, half=0):
        (hiq, hisem), (loq, losem) = (
            (("sync", "dW"), ("sync", "dW")) if pro else WQMAP[nm])
        hi_d, lo_d = wsp[nm]
        g = nw // 512
        cb = half * 8 * 512
        gb = half * 8
        for dt in range(8):
            b.step(hiq, lambda e, dt=dt, nw=nw, hi_d=hi_d, cb=cb: e.dma_start(
                WH[:, cb + dt * nw:cb + (dt + 1) * nw],
                hi_d[dt * 128:(dt + 1) * 128, :]),
                reads=[f"{nm}h{dt}"],
                writes=[f"WBH{gb + dt * g + i}" for i in range(g)],
                dma_sem=hisem)
            b.step(loq, lambda e, dt=dt, nw=nw, lo_d=lo_d, cb=cb: e.dma_start(
                WL[:, cb + dt * nw:cb + (dt + 1) * nw],
                lo_d[dt * 128:(dt + 1) * 128, :]),
                reads=[f"{nm}l{dt}"],
                writes=[f"WBL{gb + dt * g + i}" for i in range(g)],
                dma_sem=losem)

    def proj_transposed(dst, dst_key, nw, half=0):
        combos = [(WH, U2TH, "H"), (WH, U2TL, "H"), (WL, U2TH, "L")]
        cb = half * 8 * 512
        gb = half * 8
        for mc in range(4):
            pb = PP if mc % 2 == 0 else PA
            ppk = "PP" if mc % 2 == 0 else "PA"
            for ci, (wop, xop, tag) in enumerate(combos):
                for dt in range(8):
                    b.step("tensor",
                           lambda e, mc=mc, dt=dt, nw=nw, wop=wop, xop=xop,
                           pb=pb, cb=cb, fi=(ci == 0 and dt == 0),
                           la=(ci == 2 and dt == 7): e.matmul(
                        pb[:, 0:R],
                        wop[:, cb + dt * nw + mc * 128:
                            cb + dt * nw + (mc + 1) * 128],
                        xop[:, dt * R:(dt + 1) * R],
                        start=fi, stop=la),
                        reads=[f"WB{tag}{gb + dt}",
                               "U2TH" if xop is U2TH else "U2TL"],
                        writes=[ppk])
            b.step("scalar", lambda e, mc=mc, dst=dst, pb=pb: e.copy(
                dst[:, mc * R:(mc + 1) * R], pb[:, 0:R]),
                reads=[ppk], writes=[dst_key])

    def proj_normal(dst, kpre, ntiles):
        combos = [(U2TH, WH, "H"), (U2TH, WL, "L"), (U2TL, WH, "H")]
        for i in range(ntiles):
            for nb in range(2):
                for ci, (xop, wop, tag) in enumerate(combos):
                    for dt in range(8):
                        b.step("tensor",
                               lambda e, i=i, nb=nb, dt=dt, xop=xop, wop=wop,
                               fi=(ci == 0 and dt == 0),
                               la=(ci == 2 and dt == 7): e.matmul(
                            PF[:, nb * 512:(nb + 1) * 512],
                            xop[:, dt * R + i * 128: dt * R + (i + 1) * 128],
                            wop[:, dt * D + nb * 512: dt * D + (nb + 1) * 512],
                            start=fi, stop=la),
                            reads=[f"WB{tag}{2 * dt + nb}",
                                   "U2TH" if xop is U2TH else "U2TL"],
                            writes=[f"PF{nb}"])
            b.step("scalar", lambda e, i=i, dst=dst: e.copy(
                dst[:, i * D:(i + 1) * D], PF[:, 0:1024]),
                reads=["PF0", "PF1"], writes=[f"{kpre}{i}"])

    def emit_gk(bt=None, kf=None):
        bts = [(GKT, mc * R) if bt is None else bt(mc) for mc in range(4)]
        gkk = [f"GKs{mc}" if kf is None else kf(mc) for mc in range(4)]
        combos = [(WGK1H, U2TH, "WGK1H"), (WGK1H, U2TL, "WGK1H"),
                  (WGK1L, U2TH, "WGK1L")]
        for ci, (wop, xop, wk) in enumerate(combos):
            for dt in range(8):
                b.step("tensor", lambda e, dt=dt, wop=wop, xop=xop,
                       fi=(ci == 0 and dt == 0),
                       la=(ci == 2 and dt == 7): e.matmul(
                    PP[0:LOW, 0:R], wop[:, dt * LOW:(dt + 1) * LOW],
                    xop[:, dt * R:(dt + 1) * R], start=fi, stop=la),
                    reads=[wk, "U2TH" if xop is U2TH else "U2TL"],
                    writes=["PP"])
        b.step("scalar", lambda e: e.copy(Z1T[0:LOW, 0:R], PP[0:LOW, 0:R]),
               reads=["PP"], writes=["Z1T"])
        for p in range(2):
            pk = "PP" if p == 0 else "PA"
            pt = PP if p == 0 else PA
            for h in range(2):
                mc = 2 * p + h
                b.step("tensor", lambda e, mc=mc, pt=pt, h=h: e.matmul(
                    pt[:, h * R:(h + 1) * R],
                    WGK2[:, mc * 128:(mc + 1) * 128], Z1T[:, 0:R],
                    start=True, stop=True),
                    reads=["WGK2", "Z1T", "Z1TONE"], writes=[pk])
            gt, gc = bts[2 * p]
            b.step("scalar", lambda e, pt=pt, gt=gt, gc=gc: e.activation(
                gt[:, gc:gc + 2 * R], pt[:, 0:2 * R], AF.Sigmoid),
                reads=[pk], writes=[gkk[2 * p], gkk[2 * p + 1]])
        for p in range(2):
            gt, gc = bts[2 * p]
            b.step("scalar", lambda e, gt=gt, gc=gc: e.activation(
                gt[:, gc:gc + 2 * R], gt[:, gc:gc + 2 * R], AF.Ln),
                reads=[gkk[2 * p], gkk[2 * p + 1]],
                writes=[gkk[2 * p], gkk[2 * p + 1]])
        for mc in range(4):
            gt, gc = bts[mc]
            for il in range(rt):
                b.step("vector", lambda e, gt=gt, gc=gc, il=il: e.tensor_tensor_scan(
                    gt[:, gc + il * 128: gc + (il + 1) * 128],
                    gt[:, gc + il * 128: gc + (il + 1) * 128],
                    gt[:, gc + il * 128: gc + (il + 1) * 128],
                    0.0, ALU.add, ALU.bypass),
                    reads=[gkk[mc]], writes=[gkk[mc]])

    def emit_scan_chunk(iloc, full, kt=None, vn=None, bt=None, kf=None,
                        psb=None, ptb=None, statb=None):
        kt_t, kt_k = kt if kt is not None else (KT, "KT")
        vn_t, vn_k = vn if vn is not None else (VN, "VN")
        bts = [(GKT, mc * R) if bt is None else bt(mc) for mc in range(4)]
        gkk = [f"GKs{mc}" if kf is None else kf(mc) for mc in range(4)]
        C_DCm, C_BHm, sfx = statb if statb is not None else (10, 14, "")
        BTT = [bts[mc][0] for mc in range(4)]
        colC = [bts[mc][1] + iloc * 128 + 127 for mc in range(4)]
        blb = [(slice(None), slice(bts[mc][1] + iloc * 128,
                                   bts[mc][1] + (iloc + 1) * 128))
               for mc in range(4)]
        slb = [(slice(None), slice(mc * R + iloc * 128,
                                   mc * R + (iloc + 1) * 128)) for mc in range(4)]
        if ptb is None:
            PTB = [PT[:, 0:128], PF[:, 512:640]]
            K_PTB = ["PT", "PF1"]
        else:
            PTB, K_PTB = ptb
        PAB = [PA[:, 0:512], PP[:, 0:512]]
        K_PAB = ["PA", "PP"]
        if psb is None:
            PSB = [PS_[:, 0:256], PF[:, 0:256]]
            K_PSB = ["PS", "PF0"]
        else:
            PSB, K_PSB = psb

        def stage(mcs, fn):
            for mc in mcs:
                fn(mc, mc % 2)

        def run_pair(mcs):
            stage(mcs, lambda mc, st_: b.step("scalar", lambda e, mc=mc: e.activation(
                ST[:, C_DCm + mc:C_DCm + mc + 1],
                BTT[mc][:, colC[mc]:colC[mc] + 1],
                AF.Exp, scale=1.0 / 16.0),
                reads=[gkk[mc]], writes=[f"dc{mc}{sfx}"]))
            stage(mcs, lambda mc, st_: b.step("vector", lambda e, mc=mc: e.tensor_scalar(
                ST[:, C_BHm + mc:C_BHm + mc + 1],
                BTT[mc][:, colC[mc]:colC[mc] + 1],
                1.0 / 16.0, None, ALU.mult),
                reads=[gkk[mc]], writes=[f"bh{mc}{sfx}"]))
            stage(mcs, lambda mc, st_: b.step("scalar", lambda e, mc=mc: e.activation(
                EXX[mc % 2][:, :], BTT[mc][blb[mc]], AF.Exp,
                bias=ST[:, C_BHm + mc:C_BHm + mc + 1], scale=-1.0 / 16.0),
                reads=[gkk[mc], f"bh{mc}{sfx}"], writes=[f"EXX{st_}"]))
            stage(mcs, lambda mc, st_: b.step("vector", lambda e, mc=mc: e.tensor_tensor(
                KHT[mc % 2][:, :], kt_t[slb[mc]], EXX[mc % 2][:, :], ALU.mult),
                reads=[kt_k, f"EXX{st_}"], writes=[f"KHT{st_}"]))
            stage(mcs, lambda mc, st_: b.step("tensor", lambda e, mc=mc: e.transpose(
                PTB[mc % 2], KHT[mc % 2][:, :], IDF[:, :]),
                reads=[f"KHT{st_}", "IDF"], writes=[K_PTB[st_]]))
            stage(mcs, lambda mc, st_: b.step("vector", lambda e, mc=mc: e.tensor_copy(
                KHN[mc % 2][:, :], PTB[mc % 2]),
                reads=[K_PTB[st_]], writes=[f"KHN{st_}"]))
            if full:
                stage(mcs, lambda mc, st_: b.step(
                    "tensor", lambda e, mc=mc, iloc=iloc: e.matmul(
                        PSB[mc % 2], KHN[mc % 2][:, :],
                        vn_t[:, iloc * D + mc * 256: iloc * D + (mc + 1) * 256],
                        start=True, stop=True),
                    reads=[f"KHN{st_}", f"{vn_k}{iloc}"], writes=[K_PSB[st_]]))

                def _eq(mc, st_):
                    b.step("scalar", lambda e, mc=mc: e.activation(
                        EXX[mc % 2][:, :], BTT[mc][blb[mc]], AF.Exp,
                        bias=CLNS[:, :], scale=1.0 / 16.0),
                        reads=[gkk[mc], "CLNS"], writes=[f"EXX{st_}"])
                    b.step("vector", lambda e, mc=mc: e.tensor_tensor(
                        QTE[mc % 2][:, :], QT[slb[mc]], EXX[mc % 2][:, :], ALU.mult),
                        reads=["QT", f"EXX{st_}"], writes=[f"QTE{st_}"])
                stage(mcs, _eq)
                def _qx(mc, st_):
                    for hh in range(4):
                        b.step("gpsimd", lambda e, mc=mc, hh=hh: e.tensor_tensor(
                            QX[mc % 2][:, hh * 128:(hh + 1) * 128], QTE[mc % 2][:, :],
                            QMASK4[:, hh * 128:(hh + 1) * 128], ALU.mult),
                            reads=[f"QTE{st_}", "QMASK4"], writes=[f"QX{st_}q{hh}"])
                stage(mcs, _qx)
                def _ek(mc, st_):
                    b.step("scalar", lambda e, mc=mc: e.activation(
                        EXX[mc % 2][:, :], BTT[mc][blb[mc]], AF.Exp,
                        scale=-1.0 / 16.0),
                        reads=[gkk[mc]], writes=[f"EXX{st_}"])
                    b.step("vector", lambda e, mc=mc: e.tensor_tensor(
                        KHT[mc % 2][:, :], kt_t[slb[mc]], EXX[mc % 2][:, :],
                        ALU.mult),
                        reads=[kt_k, f"EXX{st_}"], writes=[f"KHT{st_}"])
                stage(mcs, _ek)
                stage(mcs, lambda mc, st_: b.step("tensor", lambda e, mc=mc: e.matmul(
                    PAB[mc % 2], KHT[mc % 2][:, :], QX[mc % 2][:, 0:512],
                    start=True, stop=True),
                    reads=[f"KHT{st_}"] + [f"QX{st_}q{h}" for h in range(4)],
                    writes=[K_PAB[st_]]))
                stage(mcs, lambda mc, st_: b.step("vector", lambda e, mc=mc: e.tensor_tensor(
                    AM[mc % 2][:, :], PAB[mc % 2], MASK4[:, :], ALU.mult),
                    reads=[K_PAB[st_], "MASK4"], writes=[f"AM{st_}"]))
                def _hsx(mc, st_):
                    for hh in range(4):
                        b.step("gpsimd", lambda e, mc=mc, hh=hh: e.tensor_tensor(
                            HSX[mc % 2][:, hh * DV:(hh + 1) * DV],
                            HS[:, mc * DV:(mc + 1) * DV],
                            HMASK[:, hh * DV:(hh + 1) * DV], ALU.mult),
                            reads=[f"HS{mc}", "HMASK"], writes=[f"HSX{st_}q{hh}"])
                stage(mcs, _hsx)
                def _oacc(mc, st_):
                    pob = f"PO{mc // 2}"
                    b.step("tensor", lambda e, mc=mc: e.matmul(
                        PO[:, mc * 256:(mc + 1) * 256], QTE[mc % 2][:, :],
                        HSX[mc % 2][:, 0:256], start=True, stop=False),
                        reads=[f"QTE{st_}"] + [f"HSX{st_}q{h}" for h in range(4)],
                        writes=[pob])
                    for hh in range(4):
                        h = mc * 4 + hh
                        b.step("tensor", lambda e, mc=mc, hh=hh, h=h, iloc=iloc:
                               e.matmul(
                            PO[:, h * DV:(h + 1) * DV],
                            AM[mc % 2][:, hh * 128:(hh + 1) * 128],
                            vn_t[:, iloc * D + h * DV: iloc * D + (h + 1) * DV],
                            start=False, stop=(hh == 3)),
                            reads=[f"AM{st_}", f"{vn_k}{iloc}"], writes=[pob])
                    b.step("tensor", lambda e, mc=mc, iloc=iloc: e.matmul(
                        PSB[mc % 2], KHN[mc % 2][:, :],
                        vn_t[:, iloc * D + mc * 256: iloc * D + (mc + 1) * 256],
                        start=True, stop=True),
                        reads=[f"KHN{st_}", f"{vn_k}{iloc}"], writes=[K_PSB[st_]])
                    for hh in range(4):
                        b.step("vector", lambda e, mc=mc, hh=hh:
                               e.scalar_tensor_tensor(
                            HS[hh * 32:(hh + 1) * 32, mc * DV:(mc + 1) * DV],
                            HS[hh * 32:(hh + 1) * 32, mc * DV:(mc + 1) * DV],
                            ST[hh * 32:(hh + 1) * 32, C_DCm + mc:C_DCm + mc + 1],
                            PSB[mc % 2][hh * 32:(hh + 1) * 32,
                                        hh * DV:(hh + 1) * DV],
                            ALU.mult, ALU.add),
                            reads=[f"HS{mc}", f"dc{mc}{sfx}", K_PSB[st_]],
                            writes=[f"HS{mc}"])
                stage(mcs, _oacc)
            else:
                def _spro(mc, st_):
                    b.step("tensor", lambda e, mc=mc, iloc=iloc: e.matmul(
                        PSB[mc % 2], KHN[mc % 2][:, :],
                        vn_t[:, iloc * D + mc * 256: iloc * D + (mc + 1) * 256],
                        start=True, stop=True),
                        reads=[f"KHN{st_}", f"{vn_k}{iloc}"], writes=[K_PSB[st_]])
                    for hh in range(4):
                        b.step("vector", lambda e, mc=mc, hh=hh:
                               e.scalar_tensor_tensor(
                            HS[hh * 32:(hh + 1) * 32, mc * DV:(mc + 1) * DV],
                            HS[hh * 32:(hh + 1) * 32, mc * DV:(mc + 1) * DV],
                            ST[hh * 32:(hh + 1) * 32, C_DCm + mc:C_DCm + mc + 1],
                            PSB[mc % 2][hh * 32:(hh + 1) * 32,
                                        hh * DV:(hh + 1) * DV],
                            ALU.mult, ALU.add),
                            reads=[f"HS{mc}", f"dc{mc}{sfx}", K_PSB[st_]],
                            writes=[f"HS{mc}"])
                stage(mcs, _spro)

        run_pair((0, 1))
        run_pair((2, 3))

    def emit_o_epilogue(iloc):
        b.step("scalar", lambda e: e.copy(OSB[:, :], PO[:, 0:1024]),
               reads=["PO0", "PO1"], writes=["YSC"])
        b.step("vector", lambda e: e.tensor_tensor(
            OSQ[:, :], OSB[:, :], OSB[:, :], ALU.mult),
            reads=["YSC"], writes=["DSC"])
        b.step("vector", lambda e: e.tensor_reduce(
            MS16[:, :], OSQ[:, :].rearrange("p (h v) -> p h v", v=DV), AX.X,
            ALU.add), reads=["DSC"], writes=["MS16"])
        b.step("vector", lambda e: e.tensor_scalar(
            SC16[:, :], MS16[:, :], 1.0 / DV, EPS, ALU.mult, op1=ALU.add),
            reads=["MS16"], writes=["SC16"])
        b.step("scalar", lambda e: e.activation(MS16[:, :], SC16[:, :], AF.Ln),
               reads=["SC16"], writes=["MS16"])
        b.step("scalar", lambda e: e.activation(
            SC16[:, :], MS16[:, :], AF.Exp, scale=-0.5),
            reads=["MS16"], writes=["SC16"])
        b.step("scalar", lambda e, iloc=iloc: e.activation(
            GN[:, iloc * D:(iloc + 1) * D], GN[:, iloc * D:(iloc + 1) * D],
            AF.Silu), reads=[f"GN{iloc}"], writes=[f"GN{iloc}"])
        for hh in range(H):
            b.step("vector", lambda e, hh=hh: e.tensor_scalar(
                OSB[:, hh * DV:(hh + 1) * DV], OSB[:, hh * DV:(hh + 1) * DV],
                SC16[:, hh:hh + 1], None, ALU.mult),
                reads=["YSC", "SC16"], writes=["YSC"])
        b.step("vector", lambda e, iloc=iloc: e.tensor_tensor(
            OF2[:, iloc * D:(iloc + 1) * D], OSB[:, :],
            GN[:, iloc * D:(iloc + 1) * D], ALU.mult),
            reads=["YSC", f"GN{iloc}"], writes=[f"VN{iloc}"])

    def emit_wo(iloc, res_tile, res_key):
        for half in range(2):
            for j in range(4):
                dt = half * 4 + j
                b.step("tensor", lambda e, dt=dt, j=j, iloc=iloc: e.transpose(
                    PT[:, j * 128:(j + 1) * 128],
                    OF2[:, iloc * D + dt * 128: iloc * D + (dt + 1) * 128],
                    IDF[:, :]), reads=[f"VN{iloc}", "IDF"], writes=["PT"])
            b.step("scalar", lambda e, half=half: e.copy(
                OFTH[:, half * 512:(half + 1) * 512], PT[:, :]),
                reads=["PT"], writes=[f"OFT{half}"])
            b.step("vector", lambda e, half=half: e.tensor_tensor(
                OFTL[:, half * 512:(half + 1) * 512], PT[:, :],
                OFTH[:, half * 512:(half + 1) * 512], ALU.subtract),
                reads=["PT"], writes=[f"OFT{half}"])
        combos = [(OFTH, WH, "H"), (OFTH, WL, "L"), (OFTL, WH, "H")]
        for nb in range(2):
            for ci, (xop, wop, tag) in enumerate(combos):
                for dt in range(8):
                    b.step("tensor",
                           lambda e, nb=nb, dt=dt, xop=xop, wop=wop,
                           fi=(ci == 0 and dt == 0),
                           la=(ci == 2 and dt == 7): e.matmul(
                        PF[:, nb * 512:(nb + 1) * 512],
                        xop[:, dt * 128:(dt + 1) * 128],
                        wop[:, dt * D + nb * 512: dt * D + (nb + 1) * 512],
                        start=fi, stop=la),
                        reads=[f"OFT{dt // 4}", f"WB{tag}{2 * dt + nb}"],
                        writes=[f"PF{nb}"])
        b.step("vector", lambda e: e.tensor_tensor(
            res_tile, res_tile, PF[:, 0:1024], ALU.add),
            reads=[res_key, "PF0", "PF1"], writes=[res_key])

    b.step("gpsimd", lambda e: e.iota(PIDXI[:, :], [[0, 1]], channel_multiplier=1),
           writes=["PIDXI"])
    b.step("gpsimd", lambda e: e.iota(COLI[:, :], [[0, 4], [1, 128]],
                                      channel_multiplier=0), writes=["COLI"])
    b.step("gpsimd", lambda e: e.iota(CB32[:, 0:512], [[32, 4], [0, 128]],
                                      channel_multiplier=0), writes=["CB32"])
    b.step("gpsimd", lambda e: e.iota(CB32H[:, 0:256], [[32, 4], [0, 64]],
                                      channel_multiplier=0), writes=["CB32H"])
    b.step("gpsimd", lambda e: e.memset(HS[:, :], 0.0),
           writes=["HS0", "HS1", "HS2", "HS3"])
    b.step("gpsimd", lambda e: e.memset(CLNS[:, :], LN_S), writes=["CLNS"])
    b.step("vector", lambda e: e.tensor_copy(PIDX[:, :], PIDXI[:, :]),
           reads=["PIDXI"], writes=["PIDX"])
    b.step("vector", lambda e: e.tensor_scalar(
        MASK4[:, :], COLI[:, :], PIDX[:, :], None, ALU.is_ge),
        reads=["COLI", "PIDX"], writes=["MASK4"])
    b.step("vector", lambda e: e.tensor_scalar(
        IDF[:, :], COLI[:, 0:128], PIDX[:, :], None, ALU.is_equal),
        reads=["COLI", "PIDX"], writes=["IDF"])
    b.step("vector", lambda e: e.tensor_copy(IDFB[:, :], IDF[:, :]),
           reads=["IDF"], writes=["IDFB"])

    def _blockmask(dst, src, src_key, n, dst_key):
        b.step("vector", lambda e: e.tensor_scalar(
            YSC[:, 0:n], src, PIDX[:, :], None, ALU.subtract),
            reads=[src_key, "PIDX"], writes=["YSC"])
        b.step("vector", lambda e: e.tensor_scalar(
            DSC[:, 0:n], YSC[:, 0:n], -31.5, None, ALU.is_ge),
            reads=["YSC"], writes=["DSC"])
        b.step("vector", lambda e: e.tensor_scalar(
            T1SC[:, 0:n], YSC[:, 0:n], 0.5, None, ALU.is_le),
            reads=["YSC"], writes=["T1SC"])
        b.step("vector", lambda e: e.tensor_tensor(
            dst, DSC[:, 0:n], T1SC[:, 0:n], ALU.mult),
            reads=["DSC", "T1SC"], writes=[dst_key])

    _blockmask(QMASK4[:, :], CB32[:, 0:512], "CB32", 512, "QMASK4")
    _blockmask(HMASK[:, :], CB32H[:, 0:256], "CB32H", 256, "HMASK")

    for dt in range(8):
        b.step("sync", lambda e, dt=dt: e.dma_start(
            WGK1[:, dt * LOW:(dt + 1) * LOW], wgk1_d[dt * 128:(dt + 1) * 128, :]),
            writes=["WGK1"], dma_sem="dM")
    b.step("sync", lambda e: e.dma_start(WGK2[0:LOW, :], wgk2_d[:, :]),
           writes=["WGK2"], dma_sem="dM")
    b.step("sync", lambda e: e.dma_start(
        WGK2[LOW:LOW + 1, :], bgk2_d[:].rearrange("(o n) -> o n", o=1)),
        writes=["WGK2"], dma_sem="dM")
    b.step("gpsimd", lambda e: e.memset(Z1T[:, :], 1.0),
           writes=["Z1T", "Z1TONE"])
    for c4 in range(4):
        b.step("sync", lambda e, c4=c4: e.dma_start(
            NEGB[:, c4:c4 + 1],
            bgk2_d[c4 * 128:(c4 + 1) * 128].rearrange("(p o) -> p o", o=1)),
            writes=["NEGB"], dma_sem="dM")
    for half in range(2):
        b.step("sync", lambda e, half=half: e.dma_start(
            GNP[half * 64:(half + 1) * 64, 0:1],
            gnw_d[:].rearrange("(p o) -> p o", o=1)),
            writes=["GNP"], dma_sem="dM")
    b.step("vector", lambda e: e.tensor_copy(WGK1H[:, :], WGK1[:, :]),
           reads=["WGK1"], writes=["WGK1H"])
    b.step("vector", lambda e: e.tensor_tensor(
        WGK1L[:, :], WGK1[:, :], WGK1H[:, :], ALU.subtract),
        reads=["WGK1", "WGK1H"], writes=["WGK1L"])
    for dt in range(8):
        b.step("scalar", lambda e, dt=dt: e.dma_start(
            STG[:, :], wf_d[dt * 128:(dt + 1) * 128, :]),
            writes=["STG"], dma_sem="dXq")
        for h4 in range(2):
            for j in range(4):
                nch = h4 * 4 + j
                b.step("tensor", lambda e, nch=nch, j=j: e.transpose(
                    PT[:, j * 128:(j + 1) * 128],
                    STG[:, nch * 128:(nch + 1) * 128], IDF[:, :]),
                    reads=["STG", "IDF"], writes=["PT"])
            b.step("vector", lambda e: e.tensor_reduce(
                ST[:, 16:20], PT[:, 0:512].rearrange("p (n x) -> p n x", x=128),
                AX.X, ALU.max, apply_absolute_value=True),
                reads=["PT"], writes=["st20"])
            if dt == 0:
                b.step("vector", lambda e, h4=h4: e.tensor_copy(
                    AMX8[:, h4 * 4:(h4 + 1) * 4], ST[:, 16:20]),
                    reads=["st20"], writes=[f"AMX{h4}"])
            else:
                b.step("vector", lambda e, h4=h4: e.tensor_tensor(
                    AMX8[:, h4 * 4:(h4 + 1) * 4], AMX8[:, h4 * 4:(h4 + 1) * 4],
                    ST[:, 16:20], ALU.max),
                    reads=["st20", f"AMX{h4}"], writes=[f"AMX{h4}"])
    b.step("vector", lambda e: e.tensor_scalar(
        RS8[:, 0:8], AMX8[:, 0:8], 1e-12, 1.0 / 127.0, ALU.max, op1=ALU.mult),
        reads=[f"AMX{n}" for n in range(2)], writes=["RS8"])
    b.step("vector", lambda e: e.reciprocal(AMX8[:, 0:8], RS8[:, 0:8]),
           reads=["RS8"], writes=["AMXR"])
    b.step("vector", lambda e: e.tensor_scalar(
        RS8[:, 0:8], RS8[:, 0:8], 0.5, None, ALU.mult),
        reads=["RS8"], writes=["RS8"])
    b.step("gpsimd", lambda e: e.memset(SCD[0:1, 0:128], 1.0), writes=["SCD"])
    b.step("tensor", lambda e: e.transpose(PT[0:8, 0:128], AMX8[:, 0:8],
                                           IDF[:, :]),
           reads=["AMXR", "IDF"], writes=["PT"])
    b.step("vector", lambda e: e.tensor_copy(SCT[0:8, :], PT[0:8, 0:128]),
           reads=["PT"], writes=["SCT"])
    b.step("sync", lambda e: e.dma_start(YSC[0:1, 0:D], SCT[0:8, 0:128]),
           reads=["SCT"], writes=["YSC"], dma_sem="dS")
    b.step("tensor", lambda e: e.transpose(PT[0:8, 0:128], RS8[:, 0:8],
                                           IDF[:, :]),
           reads=["RS8", "IDF"], writes=["PT"])
    b.step("vector", lambda e: e.tensor_copy(SCT[0:8, :], PT[0:8, 0:128]),
           reads=["PT"], writes=["SCT"])
    b.step("sync", lambda e: e.dma_start(DSC[0:1, 0:D], SCT[0:8, 0:128]),
           reads=["SCT"], writes=["DSC"], dma_sem="dS")
    for nb in range(2):
        b.step("tensor", lambda e, nb=nb: e.matmul(
            PF[:, nb * 512:(nb + 1) * 512], SCD[0:1, 0:128],
            YSC[0:1, nb * 512:(nb + 1) * 512], start=True, stop=True),
            reads=["SCD", "YSC"], writes=[f"PF{nb}"])
    b.step("vector", lambda e: e.tensor_copy(RSCB[:, :], PF[:, 0:1024]),
           reads=["PF0", "PF1"], writes=["RSCB"])
    for nb in range(2):
        b.step("tensor", lambda e, nb=nb: e.matmul(
            PF[:, nb * 512:(nb + 1) * 512], SCD[0:1, 0:128],
            DSC[0:1, nb * 512:(nb + 1) * 512], start=True, stop=True),
            reads=["SCD", "DSC"], writes=[f"PF{nb}"])
    b.step("vector", lambda e: e.tensor_copy(ABC[:, :], PF[:, 0:1024]),
           reads=["PF0", "PF1"], writes=["ABC"])
    for dt in range(8):
        b.step("gpsimd", lambda e, dt=dt: e.dma_start(
            STG[:, :], wf_d[dt * 128:(dt + 1) * 128, :]),
            writes=["STG"], dma_sem="dW2")
        b.step("vector", lambda e: e.tensor_tensor(
            YSC[:, :], STG[:, :], RSCB[:, :], ALU.mult),
            reads=["STG", "RSCB"], writes=["YSC"])
        b.step("vector", lambda e: e.tensor_copy(ISC[:, :], YSC[:, :]),
               reads=["YSC"], writes=["ISC"])
        b.step("vector", lambda e: e.tensor_tensor(
            DSC[:, :], YSC[:, :], ISC[:, :], ALU.subtract),
            reads=["YSC", "ISC"], writes=["DSC"])
        b.step("vector", lambda e: e.tensor_scalar(
            C2SC[:, :], DSC[:, :], -0.5, None, ALU.is_lt),
            reads=["DSC"], writes=["C2SC"])
        b.step("vector", lambda e: e.scalar_tensor_tensor(
            T1SC[:, :], DSC[:, :], 0.5, ISC[:, :], ALU.is_gt, ALU.add),
            reads=["DSC", "ISC"], writes=["T1SC"])
        b.step("vector", lambda e, dt=dt: e.tensor_tensor(
            QFFN[:, dt * D:(dt + 1) * D], T1SC[:, :], C2SC[:, :], ALU.subtract),
            reads=["T1SC", "C2SC"], writes=["QFFN"])
    for i in range(ntm - 2 * rt):
        b.step("scalar", lambda e, i=i: e.dma_start(
            XR[:, i * D:(i + 1) * D], x_d[(ntp + i) * 128:(ntp + i + 1) * 128, :]),
            writes=[f"XR{i}"], dma_sem="dX")
    for nm, dram, ncol in (("wk", wk_d, KD), ("wv", wv_d, D)):
        hi_d, lo_d = wsp[nm]
        for dt in range(8):
            b.step("sync", lambda e, dt=dt, dram=dram, ncol=ncol: e.dma_start(
                STG[:, 0:ncol], dram[dt * 128:(dt + 1) * 128, :]),
                writes=["STG"], dma_sem="dW")
            b.step("scalar", lambda e, ncol=ncol: e.copy(
                T1SC[:, 0:ncol], STG[:, 0:ncol]),
                reads=["STG"], writes=["T1SC"])
            b.step("vector", lambda e, ncol=ncol: e.tensor_tensor(
                C2SC[:, 0:ncol], STG[:, 0:ncol], T1SC[:, 0:ncol], ALU.subtract),
                reads=["STG", "T1SC"], writes=["C2SC"])
            b.step("sync", lambda e, dt=dt, hi_d=hi_d, ncol=ncol: e.dma_start(
                hi_d[dt * 128:(dt + 1) * 128, :], T1SC[:, 0:ncol]),
                reads=["T1SC"], writes=[f"{nm}h{dt}"], dma_sem="dS")
            b.step("sync", lambda e, dt=dt, lo_d=lo_d, ncol=ncol: e.dma_start(
                lo_d[dt * 128:(dt + 1) * 128, :], C2SC[:, 0:ncol]),
                reads=["C2SC"], writes=[f"{nm}l{dt}"], dma_sem="dS")
    b.flush("setup")
    setup_ctx.close()

    n_pro_rounds = ntp // rt
    n_main_rounds = ntm // rt

    for nm, dram, ncol in (("wq", wq_d, KD), ("wg", wg_d, D), ("wo", wo_d, D)):
        hi_d, lo_d = wsp[nm]
        for dt in range(8):
            b.step("gpsimd", lambda e, dt=dt, dram=dram, ncol=ncol: e.dma_start(
                SCD[:, 0:ncol], dram[dt * 128:(dt + 1) * 128, :]),
                writes=["SCD"], dma_sem="dW2")
            if nm == "wo":
                b.step("gpsimd", lambda e, ncol=ncol: e.tensor_scalar(
                    SCD[:, 0:ncol], SCD[:, 0:ncol], GNP[:, 0:1], None,
                    ALU.mult), reads=["SCD", "GNP"], writes=["SCD"])
            b.step("gpsimd", lambda e, ncol=ncol: e.tensor_copy(
                OFTH[:, 0:ncol], SCD[:, 0:ncol]),
                reads=["SCD"], writes=["OFT0", "OFT1"])
            b.step("gpsimd", lambda e, ncol=ncol: e.tensor_tensor(
                OFTL[:, 0:ncol], SCD[:, 0:ncol], OFTH[:, 0:ncol],
                ALU.subtract),
                reads=["SCD", "OFT0", "OFT1"], writes=["OFT0", "OFT1"])
            b.step("gpsimd", lambda e, dt=dt, hi_d=hi_d, ncol=ncol: e.dma_start(
                hi_d[dt * 128:(dt + 1) * 128, :], OFTH[:, 0:ncol]),
                reads=["OFT0", "OFT1"], writes=[f"{nm}h{dt}"], dma_sem="dS2")
            b.step("gpsimd", lambda e, dt=dt, lo_d=lo_d, ncol=ncol: e.dma_start(
                lo_d[dt * 128:(dt + 1) * 128, :], OFTL[:, 0:ncol]),
                reads=["OFT0", "OFT1"], writes=[f"{nm}l{dt}"], dma_sem="dS2")

    quanted = set()
    applied = set()

    def apply_b(nxt):
        if nxt >= n_main_rounds or nxt not in quanted or nxt in applied:
            return
        applied.add(nxt)
        for il in range(rt):
            t = nxt * rt + il
            emit_ffn_apply(XR[:, t * D:(t + 1) * D], f"XR{t}", FQB(il),
                           slc=(18 + il, f"stq{il}"), altp=True)
    dle_rnd = n_pro_rounds + min(2, (ntm - 2 * rt) // rt)

    def quant_next(nxt, at_rnd):
        if nxt >= n_main_rounds or nxt in quanted:
            return
        for il in range(rt):
            t = nxt * rt + il
            if t >= ntm - 2 * rt and at_rnd < dle_rnd:
                return
        quanted.add(nxt)
        for il in range(rt):
            t = nxt * rt + il
            emit_ffn_quant(XR[:, t * D:(t + 1) * D], f"XR{t}", FQB(il),
                           slc=(18 + il, f"stq{il}"))
    for rnd in range(n_pro_rounds + n_main_rounds):
        pro = rnd < n_pro_rounds
        if pro:
            base = ntm - 2 * rt + (rnd % 2) * rt
            res = [(XR[:, (base + il) * D:(base + il + 1) * D],
                    f"XR{base + il}") for il in range(rt)]
            for il in range(rt):
                gi = rnd * rt + il
                b.step("scalar", lambda e, il=il, gi=gi, base=base: e.dma_start(
                    XR[:, (base + il) * D:(base + il + 1) * D],
                    x_d[gi * 128:(gi + 1) * 128, :]),
                    writes=[f"XR{base + il}"], dma_sem="dX")
        else:
            mrnd = rnd - n_pro_rounds
            if rnd == n_pro_rounds + min(2, (ntm - 2 * rt) // rt):
                for i in range(ntm - 2 * rt, ntm):
                    b.step("scalar", lambda e, i=i: e.dma_start(
                        XR[:, i * D:(i + 1) * D],
                        x_d[(ntp + i) * 128:(ntp + i + 1) * 128, :]),
                        writes=[f"XR{i}"], dma_sem="dX2")
            res = [(XR[:, (mrnd * rt + il) * D:(mrnd * rt + il + 1) * D],
                    f"XR{mrnd * rt + il}") for il in range(rt)]

        if pro:
            for tile, key in res:
                emit_ffn(tile, key)
        else:
            quant_next(rnd - n_pro_rounds, rnd)
            if (rnd - n_pro_rounds) not in applied:
                applied.add(rnd - n_pro_rounds)
                for il, (tile, key) in enumerate(res):
                    emit_ffn_apply(tile, key, FQB(il),
                                   slc=(18 + il, f"stq{il}"), altp=True)
        for il, (tile, key) in enumerate(res):
            emit_ln2_u2t(tile, key, il)
        if pro and rnd % 2 == 1:
            ktp, vnp = (QT, "QT"), (GN, "GN")
            btp = lambda mc: (AM[mc // 2], (mc % 2) * R)
            kfp = lambda mc: f"GKb{mc}"
            stp = (34, 38, "b")
        else:
            ktp, vnp, btp, kfp, stp = (KT, "KT"), (VN, "VN"), None, None, None
        if pro:
            psbp = ([PO[:, 0:256], PO[:, 512:768]], ["PO0", "PO1"])
            ptbp = ([PO[:, 256:384], PO[:, 768:896]], ["PO0", "PO1"])
        else:
            psbp = ptbp = None
        emit_gk(btp, kfp)
        if not pro:
            load_weight("wq", KD)
            load_weight("wk", KD, half=1)
            proj_transposed(QT, "QT", KD)
            proj_transposed(ktp[0], ktp[1], KD, half=1)
            quant_next(rnd - n_pro_rounds + 1, rnd)
        else:
            load_weight("wk", KD, pro=pro)
            proj_transposed(ktp[0], ktp[1], KD)
        load_weight("wv", D, pro=pro)
        proj_normal(vnp[0], vnp[1], rt)
        if not pro:
            load_weight("wg", D)
            proj_normal(GN, "GN", rt)
        if not pro:
            load_weight("wo", D)
        for il in range(rt):
            emit_scan_chunk(il, full=(not pro), kt=ktp, vn=vnp, bt=btp,
                            kf=kfp, psb=psbp, ptb=ptbp, statb=stp)
            if not pro:
                emit_o_epilogue(il)
        if pro and rnd == n_pro_rounds - 1:
            quant_next(0, rnd)
        if not pro:
            quant_next(rnd - n_pro_rounds + 1, rnd)
            mrnd = rnd - n_pro_rounds
            emit_wo(0, res[0][0], res[0][1])
            SB_[0] = 24
            emit_ffn_quant(res[0][0], res[0][1], slc=(20, "stq20"))
            SB_[0] = 0
            emit_wo(1, res[1][0], res[1][1])
            SB_[0] = 24
            emit_ffn_apply(res[0][0], res[0][1], slc=(20, "stq20"))
            b.step("sync", lambda e, gi=mrnd * rt, tile=res[0][0]: e.dma_start(
                y_d[gi * 128:(gi + 1) * 128, :], tile),
                reads=[res[0][1]], writes=[f"Y{mrnd * rt}"], dma_sem="dO")
            emit_ffn_quant(res[1][0], res[1][1], slc=(21, "stq21"))
            SB_[0] = 0
            apply_b(mrnd + 1)
            SB_[0] = 24
            emit_ffn_apply(res[1][0], res[1][1], slc=(21, "stq21"))
            SB_[0] = 0
            b.step("sync", lambda e, gi=mrnd * rt + 1, tile=res[1][0]:
                   e.dma_start(
                y_d[gi * 128:(gi + 1) * 128, :], tile),
                reads=[res[1][1]], writes=[f"Y{mrnd * rt + 1}"], dma_sem="dO")
    b.step("sync", lambda e: e.nop(), waits=[("dO", 16 * ntm)])
    b.flush("rounds")
    ctx.close()
    return nc


def kernel(L0_attn_x, L0_attn_Wq, L0_attn_Wk, L0_attn_Wv, L0_attn_Wgk1,
           L0_attn_Wgk2, L0_attn_bgk2, L0_attn_Wg, L0_attn_gnw, L0_attn_Wo,
           L0_ffn_rhs):
    NTM, NTP = 16, 6
    TCK = NTM * 128
    W = NTP * 128
    x = np.ascontiguousarray(np.asarray(L0_attn_x, dtype=np.float32).reshape(-1, D))
    T = x.shape[0]
    xp = np.concatenate([np.zeros((W, D), np.float32), x], axis=0)
    nc = build_program(ntm=NTM, ntp=NTP, rt=2)
    ws = {
        "wq": L0_attn_Wq, "wk": L0_attn_Wk, "wv": L0_attn_Wv,
        "wgk1": L0_attn_Wgk1, "wgk2": L0_attn_Wgk2, "bgk2": L0_attn_bgk2,
        "wg": L0_attn_Wg, "gnw": L0_attn_gnw, "wo": L0_attn_Wo, "wf": L0_ffn_rhs,
    }
    ws = {k: np.ascontiguousarray(np.asarray(v, dtype=np.float32))
          for k, v in ws.items()}
    in_maps = []
    for c in range(8):
        sl = xp[c * TCK: c * TCK + TCK + W]
        in_maps.append({"x": np.ascontiguousarray(sl), **ws})
    res = run_bass_kernel_spmd(nc, in_maps, core_ids=list(range(8)))
    out = np.concatenate([res.results[c]["y"] for c in range(8)], axis=0)
    return out.reshape(1, T, D)
